# version 05
# baseline (speedup 1.0000x reference)
import contextlib
import numpy as np
import concourse.bass as bass
import concourse.mybir as mybir
from concourse.bass_utils import run_bass_kernel_spmd

F32 = mybir.dt.float32
BF16 = mybir.dt.bfloat16
AF = mybir.ActivationFunctionType
OP = mybir.AluOpType
AX = mybir.AxisListType

D = 2048
T = 2048
NCORE = 8
HIDS = [1024, 2048, 3072, 4096, 1024, 2048, 3072, 4096]
LHID = 1024
COST_LAMBDA = 1e-7
EPS = 1e-5
INV_S64 = float(np.float64(1.0) / np.sqrt(np.float64(2048.0)))
INV_S_HI = float(np.float32(INV_S64))
INV_S_LO = float(np.float32(INV_S64 - np.float64(np.float32(INV_S64))))

ENGINES = ("tensor", "vector", "scalar", "sync", "gpsimd")

DEBUG_TAPS = ()
BUILD_PHASES = 99


class Sem:
    __slots__ = ("h", "n")

    def __init__(self, h):
        self.h = h
        self.n = 0

    def inc(self, k=1):
        self.n += k
        return self.n


class K:

    def __init__(self):
        self.nc = bass.Bass("TRN2", target_bir_lowering=False)
        self.q = {e: [] for e in ENGINES}
        self.es = contextlib.ExitStack()
        self.dma_sems = []
        self.sTN = Sem(self.nc.alloc_semaphore("sTN"))
        self.sVE = Sem(self.nc.alloc_semaphore("sVE"))
        self.sSC = Sem(self.nc.alloc_semaphore("sSC"))
        self.ptr_last = [0, 0]
        self.ptr = None
        self.ptr_view = None
        self.tg_bank = 0
        self.dirty = {e: False for e in ENGINES}
        self.fencebuf = self.sb("fencebuf", [1, 8], F32)
        self.op("vector", lambda e: e.memset(self.fencebuf[:, :], 0.0),
                inc=self.sVE)

    def sb(self, name, shape, dtype, stack=None):
        g = self.nc.sbuf_tensor(name, list(shape), dtype)
        h = (stack or self.es).enter_context(g)
        return h.ap()

    def ps(self, name, shape, dtype=F32, stack=None):
        g = self.nc.psum_tensor(name, list(shape), dtype)
        h = (stack or self.es).enter_context(g)
        return h.ap()

    def emit(self, eng, fn):
        self.q[eng].append(fn)

    def wait(self, eng, sem, val):
        if val and val > 0:
            self.emit(eng, lambda e, s=sem.h, v=val: e.wait_ge(s, v))

    def op(self, eng, fn, inc=None, inc_k=1):
        if inc is None:
            self.dirty[eng] = True
            self.emit(eng, fn)
            return None
        self.dirty[eng] = False
        v = inc.inc(inc_k)
        self.emit(eng, lambda e, f=fn, s=inc.h, k=inc_k: f(e).then_inc(s, k))
        return v

    def fence(self, eng):
        sem = {"vector": self.sVE, "scalar": self.sSC}[eng]
        if eng == "vector":
            return self.op(eng, lambda e: e.tensor_copy(
                self.fencebuf[0:1, 2:3], self.fencebuf[0:1, 0:1]), inc=sem)
        return self.op(eng, lambda e: e.copy(
            self.fencebuf[0:1, 3:4], self.fencebuf[0:1, 0:1]), inc=sem)

    def barrier(self):
        assert not self.dirty["tensor"], \
            "barrier: tensor stream must end with an inc-carrying instruction"
        for e in ("vector", "scalar"):
            if self.dirty[e]:
                self.fence(e)
        tt, vt, sct = self.sTN.n, self.sVE.n, self.sSC.n
        dw = [(s, s.n) for s in self.dma_sems if s.n > 0]
        for e in ENGINES:
            self.wait(e, self.sTN, tt)
            self.wait(e, self.sVE, vt)
            self.wait(e, self.sSC, sct)
            for s, v in dw:
                self.wait(e, s, v)

    def dma_sem(self, name):
        s = Sem(self.nc.alloc_semaphore(name))
        self.dma_sems.append(s)
        return s

    def load(self, dst, src, sem, queue="sync"):
        self.wait(queue, sem, sem.n)
        return self.op(queue, lambda e, d=dst, s=src: e.dma_start(out=d, in_=s),
                       inc=sem, inc_k=16)

    def store(self, dst, src, sem):
        self.wait("gpsimd", sem, sem.n)
        return self.op("gpsimd", lambda e, d=dst, s=src: e.dma_start(out=d, in_=s),
                       inc=sem, inc_k=16)

    def vec(self, fn, inc=None):
        self.wait("vector", self.sVE, self.sVE.n)
        return self.op("vector", fn, inc=inc or self.sVE)

    def sca(self, fn, inc=None):
        self.wait("scalar", self.sSC, self.sSC.n)
        return self.op("scalar", fn, inc=inc or self.sSC)

    def split(self, hi, lo, src, inc=None):
        self.vec(lambda e, d=hi, s=src: e.tensor_copy(d, s))
        return self.vec(lambda e, d=lo, s=src, h=hi:
                        e.tensor_tensor(d, s, h, OP.subtract), inc=inc)

    def mm3(self, psum, whi, wlo, xhi, xlo, start, stop, inc=None):
        self.op("tensor", lambda e, p=psum, w=whi, x=xhi, st=start:
                e.matmul(p, w, x, start=st, stop=False, skip_group_check=True))
        self.op("tensor", lambda e, p=psum, w=whi, x=xlo:
                e.matmul(p, w, x, start=False, stop=False, skip_group_check=True))
        return self.op("tensor", lambda e, p=psum, w=wlo, x=xhi, sp=stop:
                       e.matmul(p, w, x, start=False, stop=sp,
                                skip_group_check=True), inc=inc)

    def tgroup(self, srcs, dst, ident):
        bank = self.tg_bank
        self.tg_bank ^= 1
        n = len(srcs)
        self.wait("tensor", self.sVE, self.ptr_last[bank])
        tv = None
        for j, s in enumerate(srcs):
            tv = self.op("tensor",
                         lambda e, p=self.ptr[:, bank * 1024 + j * 128:
                                              bank * 1024 + (j + 1) * 128],
                         s2=s, i=ident: e.transpose(p, s2, i),
                         inc=self.sTN if j == n - 1 else None)
        self.wait("vector", self.sTN, tv)
        ev = self.vec(lambda e, d=dst,
                      s=self.ptr_view[:, bank, 0:n, :]: e.tensor_copy(d, s))
        self.ptr_last[bank] = ev
        return tv, ev


def build_program():
    k = K()
    nc = k.nc
    dr = {}

    def din(name, shape, dtype=F32):
        dr[name] = nc.dram_tensor(name, list(shape), dtype, kind="ExternalInput").ap()
        return dr[name]

    g = dict(
        x=din("x", [T, D]),
        w_qkv=din("w_qkv", [3 * D, D]), b_qkv=din("b_qkv", [1, 3 * D]),
        w_out=din("w_out", [D, D]), b_out=din("b_out", [1, D]),
        down_w=din("down_w", [D, LHID]), down_b=din("down_b", [1, LHID]),
        up_w=din("up_w", [LHID, D]), up_b=din("up_b", [1, D]),
        router_w=din("router_w", [D, 8]), router_b=din("router_b", [1, 8]),
        shared_w1=din("shared_w1", [LHID, LHID]), shared_b1=din("shared_b1", [1, LHID]),
        shared_w2=din("shared_w2", [LHID, LHID]), shared_b2=din("shared_b2", [1, LHID]),
        core_w=din("core_w", [D, D]), core_b=din("core_b", [1, D]),
        c_identbf=din("c_identbf", [128, 128], BF16),
        c_ident32=din("c_ident32", [128, 128]),
        c_ones_bf=din("c_ones_bf", [1, 128], BF16),
        c_sel=din("c_sel", [9, 8 * 128], BF16),
        c_ones512=din("c_ones512", [1, 512]),
        c_rank8=din("c_rank8", [128, 8]),
        c_neglc_hi=din("c_neglc_hi", [1, 8]),
        c_neglc_lo=din("c_neglc_lo", [1, 8]),
    )
    for i, he in enumerate(HIDS):
        g[f"e{i}_w1"] = din(f"e{i}_w1", [LHID, he])
        g[f"e{i}_b1"] = din(f"e{i}_b1", [1, he])
        g[f"e{i}_w2"] = din(f"e{i}_w2", [he, LHID])
        g[f"e{i}_b2"] = din(f"e{i}_b2", [1, LHID])
    g["out"] = nc.dram_tensor("out", [T, D], F32, kind="ExternalOutput").ap()

    def scr(name, shape, dtype):
        kind = "ExternalOutput" if name in DEBUG_TAPS else "Internal"
        g[name] = nc.dram_tensor(name, list(shape), dtype, kind=kind).ap()
        return g[name]

    scr("q_scr_hi", [16, 128, T], BF16)
    scr("q_scr_lo", [16, 128, T], BF16)
    scr("k_scr_hi", [16, 128, T], BF16)
    scr("k_scr_lo", [16, 128, T], BF16)
    scr("v_scr_hi", [16, T, 128], BF16)
    scr("v_scr_lo", [16, T, 128], BF16)
    scr("ctx_scr_hi", [16, 128, T], BF16)
    scr("ctx_scr_lo", [16, 128, T], BF16)
    scr("u_scr", [T, D], F32)
    scr("bout_scr", [2, D], BF16)
    scr("sc_down_hi", [8, 128, 2048], BF16)
    scr("sc_down_lo", [8, 128, 2048], BF16)
    scr("sc_w1_hi", [168, 128, 1024], BF16)
    scr("sc_w1_lo", [168, 128, 1024], BF16)
    scr("sc_w2_hi", [84, 128, 2048], BF16)
    scr("sc_w2_lo", [84, 128, 2048], BF16)
    scr("sc_up_hi", [16, 128, 1024], BF16)
    scr("sc_up_lo", [16, 128, 1024], BF16)
    scr("sc_core_hi", [16, 128, 2048], BF16)
    scr("sc_core_lo", [16, 128, 2048], BF16)

    p = {}
    for name, shape, dt_ in [
        ("identbf", [128, 128], BF16), ("ident32", [128, 128], F32),
        ("rank8", [128, 8], F32), ("ones_bf", [1, 128], BF16),
        ("sel8", [9, 8 * 128], BF16),
        ("bqkv_cols", [128, 48], F32), ("db_cols", [128, 8], F32),
        ("sb1_cols", [128, 8], F32), ("b1_cols", [128, 160], F32),
        ("ubcb_cols", [128, 16], F32),
        ("rb_eff_hi", [1, 8], BF16), ("rb_eff_lo", [1, 8], BF16),
        ("rw_hi", [128, 128], BF16), ("rw_lo", [128, 128], BF16),
        ("bmat_hi", [9, LHID], BF16), ("bmat_lo", [9, LHID], BF16),
    ]:
        p[name] = k.sb(name, shape, dt_)

    b1_off = []
    off = 0
    for he in HIDS:
        b1_off.append(off)
        off += he // 128

    phase0(k, g, p, b1_off)
    k.barrier()
    pst = make_prep(k, g, p, b1_off)
    if BUILD_PHASES >= 1:
        phase1(k, g, p)
    run_prep(k, 10 ** 9)
    k.barrier()
    pst.close()
    if BUILD_PHASES >= 2:
        phase2(k, g, p)
        k.barrier()
    if BUILD_PHASES >= 3:
        phase3(k, g, p)
        k.barrier()
    if BUILD_PHASES >= 4:
        phase4(k, g, p, b1_off)

    for s in k.dma_sems:
        if s.n:
            k.wait("gpsimd", s, s.n)
            k.wait("tensor", s, s.n)

    with nc.Block() as block:
        @block.tensor
        def _(e):
            for fn in k.q["tensor"]:
                fn(e)

        @block.vector
        def _(e):
            for fn in k.q["vector"]:
                fn(e)

        @block.scalar
        def _(e):
            for fn in k.q["scalar"]:
                fn(e)

        @block.sync
        def _(e):
            for fn in k.q["sync"]:
                fn(e)

        @block.gpsimd
        def _(e):
            for fn in k.q["gpsimd"]:
                fn(e)

    k.es.close()
    return nc


def phase0(k, g, p, b1_off):
    st = contextlib.ExitStack()
    for name, shape, dt_ in [
        ("bout_row", [1, D], F32), ("rb_eff", [1, 8], F32),
        ("brow_s", [1, 6144], F32), ("smallcol", [128, 64], F32),
    ]:
        p[name] = k.sb(name, shape, dt_, stack=st)
    sP0 = k.dma_sem("sP0")
    k.load(p["identbf"][:, :], g["c_identbf"][:, :], sP0)
    k.load(p["ident32"][:, :], g["c_ident32"][:, :], sP0)
    k.load(p["rank8"][:, :], g["c_rank8"][:, :], sP0)
    k.load(p["ones_bf"][:, :], g["c_ones_bf"][:, :], sP0)
    k.load(p["sel8"][:, :], g["c_sel"][:, :], sP0)

    p0a = k.ps("p0a", [128, 512], F32, stack=st)
    brow = p["brow_s"]

    def rows_to_cols(src_row_hbm, n, dst_cols, dst_off=0):
        lv = k.load(brow[:, 0:n * 128], src_row_hbm, sP0)
        k.wait("tensor", sP0, lv)
        tv = None
        for j in range(n):
            tv = k.op("tensor", lambda e, pp=p0a[:, j:j + 1],
                      w=brow[:, j * 128:(j + 1) * 128],
                      m=p["ident32"][0:1, 0:1], s=(j == 0), t=(j == n - 1):
                      e.matmul(pp, w, m, start=s, stop=t, skip_group_check=True),
                      inc=k.sTN if j == n - 1 else None)
        k.wait("vector", k.sTN, tv)
        ev = k.vec(lambda e, d=dst_cols[:, dst_off:dst_off + n],
                  s=p0a[:, 0:n]: e.tensor_copy(d, s), inc=k.sVE)
        k.wait("tensor", k.sVE, ev)
        k.wait("sync", k.sVE, ev)
        return ev

    rows_to_cols(g["b_qkv"][:, :], 48, p["bqkv_cols"])
    rows_to_cols(g["down_b"][:, :], 8, p["db_cols"])
    rows_to_cols(g["shared_b1"][:, :], 8, p["sb1_cols"])
    for i in range(8):
        rows_to_cols(g[f"e{i}_b1"][:, :], HIDS[i] // 128, p["b1_cols"], b1_off[i])
    rows_to_cols(g["up_b"][:, :], 16, p["ubcb_cols"])
    ev = rows_to_cols(g["core_b"][:, :], 16, p["smallcol"])
    k.wait("vector", k.sVE, ev)
    k.vec(lambda e: e.tensor_tensor(
        p["ubcb_cols"][:, 0:16], p["ubcb_cols"][:, 0:16],
        p["smallcol"][:, 0:16], OP.add))

    for name, shape, dt_ in [("bout_row_hi", [1, D], BF16),
                             ("bout_row_lo", [1, D], BF16)]:
        p[name] = k.sb(name, shape, dt_, stack=st)
    lv = k.load(p["bout_row"][:, :], g["b_out"][:, :], sP0)
    k.wait("vector", sP0, lv)
    bsp = k.split(p["bout_row_hi"][:, :], p["bout_row_lo"][:, :],
                  p["bout_row"][:, :])
    k.wait("gpsimd", k.sVE, bsp)
    sBout = k.dma_sem("sBout")
    k.store(g["bout_scr"][0:1, :], p["bout_row_hi"][:, :], sBout)
    k.store(g["bout_scr"][1:2, :], p["bout_row_lo"][:, :], sBout)

    k.load(brow[:, 0:8], g["router_b"][:, :], sP0)
    k.load(brow[:, 128:136], g["c_neglc_hi"][:, :], sP0)
    lv = k.load(brow[:, 256:264], g["c_neglc_lo"][:, :], sP0)
    k.wait("vector", sP0, lv)
    k.vec(lambda e: e.tensor_tensor(
        p["rb_eff"][:, :], brow[0:1, 0:8], brow[0:1, 128:136], OP.add))
    k.vec(lambda e: e.tensor_tensor(
        p["rb_eff"][:, :], p["rb_eff"][:, :], brow[0:1, 256:264], OP.add))
    k.split(p["rb_eff_hi"][:, :], p["rb_eff_lo"][:, :], p["rb_eff"][:, :])

    rwnat = k.sb("rwnat", [128, 128], F32, stack=st)
    rw_view = g["router_w"].rearrange("(kc pp) o -> pp kc o", pp=128)
    lv = k.load(rwnat.rearrange("pp (kc o) -> pp kc o", kc=16), rw_view, sP0)
    k.wait("vector", sP0, lv)
    k.split(p["rw_hi"][:, :], p["rw_lo"][:, :], rwnat[:, :])

    st.close()


def make_prep(k, g, p, b1_off):
    pst = contextlib.ExitStack()
    w2s = k.sb("w2s", [128, 2048], F32, stack=pst)
    w2s_hi = k.sb("w2s_hi", [128, 2048], BF16, stack=pst)
    w2s_lo = k.sb("w2s_lo", [128, 2048], BF16, stack=pst)
    b2row = k.sb("b2row", [1, LHID], F32, stack=pst)
    crow = k.sb("crow", [1, LHID], F32, stack=pst)
    prow = k.sb("prow", [1, LHID], F32, stack=pst)
    bmat = k.sb("bmat", [9, LHID], F32, stack=pst)
    gb1_cols = k.sb("gb1_cols", [128, 160], F32, stack=pst)
    gb1_hi = k.sb("gb1_hi", [128, 160], BF16, stack=pst)
    gb1_lo = k.sb("gb1_lo", [128, 160], BF16, stack=pst)
    p0b = k.ps("p0b", [1, LHID], F32, stack=pst)
    sW2 = [k.dma_sem("sW2a"), k.dma_sem("sW2b")]
    sPb = k.dma_sem("sPb")
    stPr = k.dma_sem("stPr")

    k.wait("scalar", k.sVE, k.sVE.n)
    sv = k.sca(lambda e: e.activation(
        gb1_cols[:, 0:160], p["b1_cols"][:, 0:160], AF.Gelu), inc=k.sSC)
    k.wait("vector", k.sSC, sv)
    k.split(gb1_hi[:, :], gb1_lo[:, :], gb1_cols[:, :])
    k.vec(lambda e: e.memset(crow[:, :], 0.0))

    w1base, w2base = [], []
    o1 = o2 = 0
    for he in HIDS:
        w1base.append(o1)
        w2base.append(o2)
        o1 += he // 128
        o2 += he // 256
    shared_w1base, shared_w2base = o1, o2

    st_state = {"slot": 0, "split": [0, 0], "store": [0, 0], "mm": [0, 0]}
    jobs = []

    def chunk_job(src_ap, w, dh, dl, cmms=None):
        def emit():
            sl = st_state["slot"]
            st_state["slot"] ^= 1
            k.wait("scalar", k.sVE, st_state["split"][sl])
            lv = k.load(w2s[:, sl * 1024:sl * 1024 + w], src_ap, sW2[sl],
                        queue="scalar")
            k.wait("vector", sW2[sl], lv)
            k.wait("vector", stPr, st_state["store"][sl])
            if cmms is not None:
                k.wait("vector", k.sTN, st_state["mm"][sl])
            sd = k.split(w2s_hi[:, sl * 1024:sl * 1024 + w],
                         w2s_lo[:, sl * 1024:sl * 1024 + w],
                         w2s[:, sl * 1024:sl * 1024 + w])
            st_state["split"][sl] = sd
            k.wait("gpsimd", k.sVE, sd)
            srch = w2s_hi[:, sl * 1024:sl * 1024 + w]
            srcl = w2s_lo[:, sl * 1024:sl * 1024 + w]
            if len(dh.shape) == 3:
                srch = srch.rearrange("pp (s c) -> pp s c", c=128)
                srcl = srcl.rearrange("pp (s c) -> pp s c", c=128)
            k.store(dh, srch, stPr)
            st_state["store"][sl] = k.store(dl, srcl, stPr)
            if cmms is not None:
                ei, kc, kcs = cmms
                k.wait("tensor", k.sVE, sd)
                for half in range(2):
                    slc = slice(sl * 1024 + half * 512,
                                sl * 1024 + (half + 1) * 512)
                    st_state["mm"][sl] = k.mm3(
                        p0b[:, half * 512:(half + 1) * 512],
                        gb1_hi[:, b1_off[ei] + kc:b1_off[ei] + kc + 1],
                        gb1_lo[:, b1_off[ei] + kc:b1_off[ei] + kc + 1],
                        w2s_hi[:, slc], w2s_lo[:, slc],
                        start=(kc == 0), stop=(kc == kcs - 1),
                        inc=k.sTN if half == 1 else None)
        return emit

    def expert_finish_job(ei):
        def emit():
            k.wait("vector", k.sTN, max(st_state["mm"]))
            ev = k.vec(lambda e: e.tensor_copy(prow[:, :], p0b[:, :]))
            k.wait("tensor", k.sVE, ev)
            lv = k.load(b2row[:, :], g[f"e{ei}_b2"][:, :], sPb)
            k.wait("vector", sPb, lv)
            k.vec(lambda e: e.tensor_tensor(
                crow[:, :], crow[:, :], prow[:, :], OP.add))
            k.vec(lambda e: e.tensor_tensor(
                crow[:, :], crow[:, :], b2row[0:1, :], OP.add))
            nv = k.vec(lambda e: e.tensor_scalar(
                prow[:, :], prow[:, :], -1.0, None, OP.mult))
            k.wait("sync", k.sVE, nv)
            rowlv = k.load(bmat[ei:ei + 1, :], prow[:, :], sPb)
            k.wait("vector", sPb, rowlv)
        return emit

    def finale_job():
        def emit():
            lv = k.load(b2row[:, :], g["shared_b2"][:, :], sPb)
            k.wait("vector", sPb, lv)
            nv = k.vec(lambda e: e.scalar_tensor_tensor(
                prow[:, :], b2row[0:1, :], 0.1, crow[:, :], OP.mult, OP.add))
            k.wait("sync", k.sVE, nv)
            lv = k.load(bmat[8:9, :], prow[:, :], sPb)
            k.wait("vector", sPb, lv)
            k.split(p["bmat_hi"][:, :], p["bmat_lo"][:, :], bmat[:, :])
        return emit

    for ei in range(8):
        kcs = HIDS[ei] // 128
        for kc in range(kcs):
            jsl = w2base[ei] + kc // 2
            hf = (kc % 2) * LHID
            jobs.append(chunk_job(
                g[f"e{ei}_w2"][kc * 128:(kc + 1) * 128, :], 1024,
                g["sc_w2_hi"][jsl][:, hf:hf + LHID],
                g["sc_w2_lo"][jsl][:, hf:hf + LHID],
                cmms=(ei, kc, kcs)))
        jobs.append(expert_finish_job(ei))
    jobs.append(finale_job())
    for kc in range(8):
        jsl = shared_w2base + kc // 2
        hf = (kc % 2) * LHID
        jobs.append(chunk_job(
            g["shared_w2"][kc * 128:(kc + 1) * 128, :], 1024,
            g["sc_w2_hi"][jsl][:, hf:hf + LHID],
            g["sc_w2_lo"][jsl][:, hf:hf + LHID]))
    for kc in range(16):
        dh = g["sc_down_hi"].rearrange(
            "oc pp (kk c) -> pp oc kk c", c=128)[:, :, kc, :]
        dl = g["sc_down_lo"].rearrange(
            "oc pp (kk c) -> pp oc kk c", c=128)[:, :, kc, :]
        jobs.append(chunk_job(g["down_w"][kc * 128:(kc + 1) * 128, :],
                              1024, dh, dl))
    for kc in range(8):
        dh = g["sc_w1_hi"].rearrange("s pp (kk c) -> pp s kk c", c=128)[
            :, shared_w1base:shared_w1base + 8, kc, :]
        dl = g["sc_w1_lo"].rearrange("s pp (kk c) -> pp s kk c", c=128)[
            :, shared_w1base:shared_w1base + 8, kc, :]
        jobs.append(chunk_job(g["shared_w1"][kc * 128:(kc + 1) * 128, :],
                              1024, dh, dl))
    for ei in range(8):
        he = HIDS[ei]
        for kc in range(8):
            for pc in range(he // 1024):
                s0 = w1base[ei] + pc * 8
                dh = g["sc_w1_hi"].rearrange(
                    "s pp (kk c) -> pp s kk c", c=128)[:, s0:s0 + 8, kc, :]
                dl = g["sc_w1_lo"].rearrange(
                    "s pp (kk c) -> pp s kk c", c=128)[:, s0:s0 + 8, kc, :]
                jobs.append(chunk_job(
                    g[f"e{ei}_w1"][kc * 128:(kc + 1) * 128,
                                   pc * 1024:(pc + 1) * 1024], 1024, dh, dl))
    for kc in range(8):
        for pc in range(2):
            dh = g["sc_up_hi"].rearrange(
                "s pp (kk c) -> pp s kk c", c=128)[:, pc * 8:pc * 8 + 8, kc, :]
            dl = g["sc_up_lo"].rearrange(
                "s pp (kk c) -> pp s kk c", c=128)[:, pc * 8:pc * 8 + 8, kc, :]
            jobs.append(chunk_job(
                g["up_w"][kc * 128:(kc + 1) * 128, pc * 1024:(pc + 1) * 1024],
                1024, dh, dl))
    for kc in range(16):
        for pc in range(2):
            dh = g["sc_core_hi"].rearrange(
                "s pp (kk c) -> pp s kk c", c=128)[:, pc * 8:pc * 8 + 8, kc, :]
            dl = g["sc_core_lo"].rearrange(
                "s pp (kk c) -> pp s kk c", c=128)[:, pc * 8:pc * 8 + 8, kc, :]
            jobs.append(chunk_job(
                g["core_w"][kc * 128:(kc + 1) * 128,
                            pc * 1024:(pc + 1) * 1024], 1024, dh, dl))
    k.prep_jobs = jobs
    k.prep_i = 0
    return pst


def run_prep(k, n):
    while n > 0 and k.prep_i < len(k.prep_jobs):
        k.prep_jobs[k.prep_i]()
        k.prep_i += 1
        n -= 1


def phase1(k, g, p):
    st = contextlib.ExitStack()
    x_fm_hi_t = k.sb("x_fm_hi", [128, 16 * T], BF16, stack=st)
    x_fm_lo_t = k.sb("x_fm_lo", [128, 16 * T], BF16, stack=st)
    st1 = contextlib.ExitStack()
    xtm = k.sb("xtm", [128, 2 * D], F32, stack=st1)
    xtm_hi = k.sb("xtm_hi", [128, 2 * D], BF16, stack=st1)
    xtm_lo = k.sb("xtm_lo", [128, 2 * D], BF16, stack=st1)
    k.ptr = k.ps("ptr", [128, 2048], BF16, stack=st)
    k.ptr_view = k.ptr.rearrange("pp (bk j c) -> pp bk j c", bk=2, j=8)
    k.ptr_last = [0, 0]
    identbf = p["identbf"]
    x_fm_hi = x_fm_hi_t.rearrange("pp (hc t) -> pp hc t", hc=16)
    x_fm_lo = x_fm_lo_t.rearrange("pp (hc t) -> pp hc t", hc=16)

    sX = [k.dma_sem("sXa"), k.dma_sem("sXb")]
    split_done = {}
    trans_done = {}
    for tc in range(16):
        slot = tc % 2
        if tc >= 2:
            k.wait("sync", k.sVE, split_done[tc - 2])
        lv = k.load(xtm[:, slot * D:(slot + 1) * D],
                    g["x"][tc * 128:(tc + 1) * 128, :], sX[slot])
        k.wait("vector", sX[slot], lv)
        if tc >= 2:
            k.wait("vector", k.sTN, trans_done[tc - 2])
        sd = k.split(xtm_hi[:, slot * D:(slot + 1) * D],
                     xtm_lo[:, slot * D:(slot + 1) * D],
                     xtm[:, slot * D:(slot + 1) * D])
        split_done[tc] = sd
        k.wait("tensor", k.sVE, sd)
        for gi in range(4):
            src = xtm_hi if gi < 2 else xtm_lo
            hc0 = (gi % 2) * 8
            srcs = [src[:, slot * D + (hc0 + j) * 128:slot * D + (hc0 + j + 1) * 128]
                    for j in range(8)]
            dst = (x_fm_hi if gi < 2 else x_fm_lo)[:, hc0:hc0 + 8,
                                                   tc * 128:(tc + 1) * 128]
            tv, ev = k.tgroup(srcs, dst, identbf)
            if gi == 3:
                trans_done[tc] = tv
    k.barrier()
    st1.close()

    wnat = k.sb("wnat", [128, D], F32, stack=st)
    wnat_hi = k.sb("wnat_hi", [128, D], BF16, stack=st)
    wnat_lo = k.sb("wnat_lo", [128, D], BF16, stack=st)
    wt_hi = k.sb("wt_hi", [128, D], BF16, stack=st)
    wt_lo = k.sb("wt_lo", [128, D], BF16, stack=st)
    qkstage_hi = k.sb("qkstage_hi", [128, 1024], BF16, stack=st)
    qkstage_lo = k.sb("qkstage_lo", [128, 1024], BF16, stack=st)
    vstage = k.sb("vstage", [128, 512], F32, stack=st)
    vstage_hi = k.sb("vstage_hi", [128, 512], BF16, stack=st)
    vstage_lo = k.sb("vstage_lo", [128, 512], BF16, stack=st)
    vt_hi = k.sb("vt_hi", [128, 512], BF16, stack=st)
    vt_lo = k.sb("vt_lo", [128, 512], BF16, stack=st)
    pq = k.ps("pq", [128, 2048], F32, stack=st)

    sWq = [k.dma_sem("sWqa"), k.dma_sem("sWqb")]
    stQK = [k.dma_sem("stQKa"), k.dma_sem("stQKb")]
    stV = k.dma_sem("stV")
    wsplit, wtrans, mmdone, pair_evac = {}, {}, {}, {}
    qk_store_last = [0, 0]
    v_store_last = 0
    wt_hi_v = wt_hi.rearrange("pp (hc c) -> pp hc c", hc=16)
    wt_lo_v = wt_lo.rearrange("pp (hc c) -> pp hc c", hc=16)
    for ob in range(48):
        if ob >= 1:
            k.wait("sync", k.sVE, wsplit[ob - 1])
        lv = k.load(wnat[:, :], g["w_qkv"][ob * 128:(ob + 1) * 128, :],
                    sWq[ob % 2])
        k.wait("vector", sWq[ob % 2], lv)
        if ob >= 1:
            k.wait("vector", k.sTN, wtrans[ob - 1])
        sd = k.split(wnat_hi[:, :], wnat_lo[:, :], wnat[:, :])
        wsplit[ob] = sd
        k.wait("tensor", k.sVE, sd)
        wtev = 0
        for gi in range(4):
            src = wnat_hi if gi < 2 else wnat_lo
            dstv = wt_hi_v if gi < 2 else wt_lo_v
            hc0 = (gi % 2) * 8
            srcs = [src[:, (hc0 + j) * 128:(hc0 + j + 1) * 128]
                    for j in range(8)]
            tv, wtev = k.tgroup(srcs, dstv[:, hc0:hc0 + 8, :], identbf)
            if gi == 3:
                wtrans[ob] = tv
        k.wait("tensor", k.sVE, wtev)
        for pair in range(2):
            if ob >= 1:
                k.wait("tensor", k.sVE, pair_evac.get((ob - 1, pair), 0))
            tv = None
            for hc in range(16):
                whid = wt_hi_v[:, hc, :]
                wlod = wt_lo_v[:, hc, :]
                for sub in (2 * pair, 2 * pair + 1):
                    psl = pq[:, sub * 512:(sub + 1) * 512]
                    xh = x_fm_hi[:, hc, sub * 512:(sub + 1) * 512]
                    xl = x_fm_lo[:, hc, sub * 512:(sub + 1) * 512]
                    last = (hc == 15 and sub == 2 * pair + 1)
                    tv = k.mm3(psl, whid, wlod, xh, xl,
                               start=(hc == 0), stop=(hc == 15),
                               inc=k.sTN if last else None)
            if pair == 1:
                mmdone[ob] = tv
            k.wait("vector", k.sTN, tv)
            for sub in (2 * pair, 2 * pair + 1):
                psl = pq[:, sub * 512:(sub + 1) * 512]
                bias = p["bqkv_cols"][:, ob:ob + 1]
                ss = sub % 2
                if ob < 32:
                    qh = qkstage_hi[:, ss * 512:(ss + 1) * 512]
                    ql = qkstage_lo[:, ss * 512:(ss + 1) * 512]
                    k.wait("vector", stQK[ss], qk_store_last[ss])
                    k.vec(lambda e, pp=psl, bi=bias, d=qh:
                         e.tensor_scalar(d, pp, bi, None, OP.add))
                    ev = k.vec(lambda e, pp=psl, bi=bias, d=ql, h=qh:
                              e.scalar_tensor_tensor(d, pp, bi, h,
                                                     OP.add, OP.subtract),
                              inc=k.sVE)
                    pair_evac[(ob, pair)] = ev
                    dc = ob % 16
                    scr_h = g["q_scr_hi"] if ob < 16 else g["k_scr_hi"]
                    scr_l = g["q_scr_lo"] if ob < 16 else g["k_scr_lo"]
                    k.wait("gpsimd", k.sVE, ev)
                    k.store(scr_h[dc][:, sub * 512:(sub + 1) * 512], qh, stQK[ss])
                    qk_store_last[ss] = k.store(
                        scr_l[dc][:, sub * 512:(sub + 1) * 512], ql, stQK[ss])
                else:
                    dc = ob - 32
                    k.wait("vector", stV, v_store_last)
                    ev = k.vec(lambda e, pp=psl, bi=bias:
                              e.tensor_scalar(vstage[:, :], pp, bi, None, OP.add),
                              inc=k.sVE)
                    pair_evac[(ob, pair)] = ev
                    sd2 = k.split(vstage_hi[:, :], vstage_lo[:, :], vstage[:, :])
                    k.wait("tensor", k.sVE, sd2)
                    vth = vt_hi.rearrange("pp (j c) -> pp j c", j=4)
                    vtl = vt_lo.rearrange("pp (j c) -> pp j c", j=4)
                    _, evh = k.tgroup([vstage_hi[:, j * 128:(j + 1) * 128]
                                       for j in range(4)], vth, identbf)
                    _, evl = k.tgroup([vstage_lo[:, j * 128:(j + 1) * 128]
                                       for j in range(4)], vtl, identbf)
                    k.wait("gpsimd", k.sVE, evl)
                    vh = g["v_scr_hi"][dc].rearrange("(a bb) d -> bb a d", bb=128)[
                        :, sub * 4:(sub + 1) * 4, :]
                    vl = g["v_scr_lo"][dc].rearrange("(a bb) d -> bb a d", bb=128)[
                        :, sub * 4:(sub + 1) * 4, :]
                    k.store(vh, vth, stV)
                    v_store_last = k.store(vl, vtl, stV)
        run_prep(k, 9)
    st.close()


def phase2(k, g, p):
    st = contextlib.ExitStack()
    identbf = p["identbf"]
    K_hi = k.sb("K_hi", [128, 16 * T], BF16, stack=st)
    K_lo = k.sb("K_lo", [128, 16 * T], BF16, stack=st)
    qsb_hi = k.sb("qsb_hi", [128, 2 * 16 * 128], BF16, stack=st)
    qsb_lo = k.sb("qsb_lo", [128, 2 * 16 * 128], BF16, stack=st)
    probs = k.sb("probs", [128, T], F32, stack=st)
    probs_hi = k.sb("probs_hi", [128, T], BF16, stack=st)
    probs_lo = k.sb("probs_lo", [128, T], BF16, stack=st)
    pT_hi = k.sb("pT_hi", [128, 16 * 256], BF16, stack=st)
    pT_lo = k.sb("pT_lo", [128, 16 * 256], BF16, stack=st)
    vsb_hi = k.sb("vsb_hi", [128, 2 * 2048], BF16, stack=st)
    vsb_lo = k.sb("vsb_lo", [128, 2 * 2048], BF16, stack=st)
    ctxs_hi = k.sb("ctxs_hi", [128, 256], BF16, stack=st)
    ctxs_lo = k.sb("ctxs_lo", [128, 256], BF16, stack=st)
    cols = k.sb("p2cols", [128, 8], F32, stack=st)
    psA = k.ps("psA", [128, 2048], F32, stack=st)
    psB = k.ps("psB", [128, 2048], F32, stack=st)
    KV = K_hi.rearrange("pp (dc t) -> pp dc t", dc=16)
    KL = K_lo.rearrange("pp (dc t) -> pp dc t", dc=16)
    QV = qsb_hi.rearrange("pp (sl dc c) -> pp sl dc c", sl=2, dc=16)
    QL = qsb_lo.rearrange("pp (sl dc c) -> pp sl dc c", sl=2, dc=16)
    PTV = pT_hi.rearrange("pp (kc c) -> pp kc c", kc=16)
    PTL = pT_lo.rearrange("pp (kc c) -> pp kc c", kc=16)
    VV = vsb_hi.rearrange("pp (sl kc c) -> pp sl kc c", sl=2, kc=16)
    VL = vsb_lo.rearrange("pp (sl kc c) -> pp sl kc c", sl=2, kc=16)
    k.ptr = psA[:, 0:1024].bitcast(BF16)
    k.ptr_view = k.ptr.rearrange("pp (bk j c) -> pp bk j c", bk=2, j=8)
    k.ptr_last = [0, 0]
    k.tg_bank = 0

    sK = [k.dma_sem("sKa"), k.dma_sem("sKb")]
    for dc in range(16):
        k.load(KV[:, dc, :], g["k_scr_hi"][dc], sK[dc % 2])
        k.load(KL[:, dc, :], g["k_scr_lo"][dc], sK[dc % 2])
    kload = [(s, s.n) for s in sK]
    sQ = [k.dma_sem("sQa"), k.dma_sem("sQb")]
    sV = [k.dma_sem("sVa"), k.dma_sem("sVb")]
    stC = k.dma_sem("stC")

    qload = {}
    scores_done = {}
    exp_done = {}
    ctx_mm_done = [0, 0]
    ctx_evac = [0, 0]
    ctx_store = [0, 0]
    pt_evac_last = [0]

    def load_q(s, qc):
        slot = (2 * s + qc) % 2
        q0 = s * 256 + qc * 128
        prev = scores_done.get(2 * s + qc - 2)
        if prev:
            k.wait("sync", k.sTN, prev)
        qload[2 * s + qc] = (
            k.load(QV[:, slot], g["q_scr_hi"].rearrange(
                "dc pp t -> pp dc t")[:, :, q0:q0 + 128], sQ[slot]),
            k.load(QL[:, slot], g["q_scr_lo"].rearrange(
                "dc pp t -> pp dc t")[:, :, q0:q0 + 128], sQ[slot]))

    load_q(0, 0)
    load_q(0, 1)
    for s in range(8):
        for qc in range(2):
            idx = 2 * s + qc
            slot = idx % 2
            P = psA if qc == 0 else psB
            lvh, lvl = qload[idx]
            k.wait("tensor", sQ[slot], lvl)
            if s == 0:
                for sem, v in kload:
                    k.wait("tensor", sem, v)
            if qc == 0:
                k.wait("tensor", k.sVE, max(k.ptr_last))
                if idx >= 2:
                    k.wait("tensor", k.sSC, exp_done[idx - 2])
            else:
                k.wait("tensor", k.sVE, max(ctx_evac))
                if idx >= 2:
                    k.wait("tensor", k.sSC, exp_done[idx - 2])
            tv = None
            for dc in range(16):
                for kb in range(4):
                    tv = k.mm3(P[:, kb * 512:(kb + 1) * 512],
                               QV[:, slot, dc, :], QL[:, slot, dc, :],
                               KV[:, dc, kb * 512:(kb + 1) * 512],
                               KL[:, dc, kb * 512:(kb + 1) * 512],
                               start=(dc == 0), stop=(dc == 15),
                               inc=k.sTN if (dc == 15 and kb == 3) else None)
            scores_done[idx] = tv
            if idx + 2 < 16:
                load_q((idx + 2) // 2, (idx + 2) % 2)
            k.wait("vector", k.sTN, tv)
            k.vec(lambda e, P=P: e.reduce_max(cols[:, 0:1], P, axis=AX.X))
            nm = k.vec(lambda e: e.tensor_scalar(
                cols[:, 1:2], cols[:, 0:1], -INV_S_HI, None, OP.mult))
            k.wait("scalar", k.sVE, nm)
            k.wait("scalar", k.sTN, tv)
            ev = k.sca(lambda e, P=P: e.activation(
                probs[:, :], P, AF.Exp, bias=cols[:, 1:2], scale=INV_S_HI,
                accum_out=cols[:, 2:3]))
            exp_done[idx] = ev
            k.wait("vector", k.sSC, ev)
            k.vec(lambda e: e.reciprocal(cols[:, 3:4], cols[:, 2:3]))
            k.vec(lambda e: e.tensor_scalar(
                probs_hi[:, :], probs[:, :], cols[:, 3:4], None, OP.mult))
            sd = k.vec(lambda e: e.scalar_tensor_tensor(
                probs_lo[:, :], probs[:, :], cols[:, 3:4], probs_hi[:, :],
                OP.mult, OP.subtract))
            k.wait("tensor", k.sVE, sd)
            k.wait("tensor", k.sSC, ev)
            if qc == 0:
                k.wait("vector", k.sTN, max(ctx_mm_done))
            for gi in range(4):
                srcp = probs_hi if gi < 2 else probs_lo
                dstv = PTV if gi < 2 else PTL
                kc0 = (gi % 2) * 8
                srcs = [srcp[:, (kc0 + j) * 128:(kc0 + j + 1) * 128]
                        for j in range(8)]
                dst = dstv[:, kc0:kc0 + 8, qc * 128:(qc + 1) * 128]
                tv2, ev2 = k.tgroup(srcs, dst, identbf)
            pt_evac_last[0] = ev2
        vload = [0, 0]
        if ctx_mm_done[0]:
            k.wait("sync", k.sTN, ctx_mm_done[0])
        k.load(VV[:, 0], g["v_scr_hi"][0].rearrange(
            "(kc pp) d -> pp kc d", pp=128), sV[0])
        vload[0] = k.load(VL[:, 0], g["v_scr_lo"][0].rearrange(
            "(kc pp) d -> pp kc d", pp=128), sV[0])
        for dc in range(16):
            slot = dc % 2
            if dc + 1 < 16:
                nsl = (dc + 1) % 2
                if ctx_mm_done[nsl]:
                    k.wait("sync", k.sTN, ctx_mm_done[nsl])
                k.load(VV[:, nsl], g["v_scr_hi"][dc + 1].rearrange(
                    "(kc pp) d -> pp kc d", pp=128), sV[nsl])
                vload[nsl] = k.load(VL[:, nsl], g["v_scr_lo"][dc + 1].rearrange(
                    "(kc pp) d -> pp kc d", pp=128), sV[nsl])
            ctxp = psB[:, 512 * slot + 0:512 * slot + 256]
            k.wait("tensor", sV[slot], vload[slot])
            k.wait("tensor", k.sVE, pt_evac_last[0])
            k.wait("tensor", k.sVE, ctx_evac[slot])
            tv = None
            for kc in range(16):
                tv = k.mm3(ctxp, VV[:, slot, kc, :], VL[:, slot, kc, :],
                           PTV[:, kc, :], PTL[:, kc, :],
                           start=(kc == 0), stop=(kc == 15),
                           inc=k.sTN if kc == 15 else None)
            ctx_mm_done[slot] = tv
            k.wait("vector", k.sTN, tv)
            k.wait("vector", stC, ctx_store[0])
            cse = k.split(ctxs_hi[:, :], ctxs_lo[:, :], ctxp)
            ctx_evac[slot] = cse
            k.wait("gpsimd", k.sVE, cse)
            q0 = s * 256
            k.store(g["ctx_scr_hi"][dc][:, q0:q0 + 256], ctxs_hi[:, :], stC)
            ctx_store[0] = k.store(
                g["ctx_scr_lo"][dc][:, q0:q0 + 256], ctxs_lo[:, :], stC)
    st.close()


def phase3(k, g, p):
    st = contextlib.ExitStack()
    identbf = p["identbf"]
    p["bout_row_hi"] = k.sb("p3bout_hi", [1, D], BF16, stack=st)
    p["bout_row_lo"] = k.sb("p3bout_lo", [1, D], BF16, stack=st)
    sB3 = k.dma_sem("sB3")
    k.load(p["bout_row_hi"][:, :], g["bout_scr"][0:1, :], sB3)
    k.load(p["bout_row_lo"][:, :], g["bout_scr"][1:2, :], sB3)
    wot_hi = k.sb("wot_hi", [128, 16 * 2048], BF16, stack=st)
    wot_lo = k.sb("wot_lo", [128, 16 * 2048], BF16, stack=st)
    WOH = wot_hi.rearrange("pp (dc o) -> pp dc o", dc=16)
    WOL = wot_lo.rearrange("pp (dc o) -> pp dc o", dc=16)

    st1 = contextlib.ExitStack()
    wnat = k.sb("p3wnat", [128, 2 * D], F32, stack=st1)
    wnat_hi = k.sb("p3wnat_hi", [128, 2 * D], BF16, stack=st1)
    wnat_lo = k.sb("p3wnat_lo", [128, 2 * D], BF16, stack=st1)
    k.ptr = k.ps("p3ptr", [128, 2048], BF16, stack=st1)
    k.ptr_view = k.ptr.rearrange("pp (bk j c) -> pp bk j c", bk=2, j=8)
    k.ptr_last = [0, 0]
    k.tg_bank = 0
    sWo = [k.dma_sem("sWoa"), k.dma_sem("sWob")]
    wsplit = {}
    wtrans = {}
    for oc in range(16):
        slot = oc % 2
        if oc >= 2:
            k.wait("sync", k.sVE, wsplit[oc - 2])
        lv = k.load(wnat[:, slot * D:(slot + 1) * D],
                    g["w_out"][oc * 128:(oc + 1) * 128, :], sWo[slot])
        k.wait("vector", sWo[slot], lv)
        if oc >= 2:
            k.wait("vector", k.sTN, wtrans[oc - 2])
        sd = k.split(wnat_hi[:, slot * D:(slot + 1) * D],
                     wnat_lo[:, slot * D:(slot + 1) * D],
                     wnat[:, slot * D:(slot + 1) * D])
        wsplit[oc] = sd
        k.wait("tensor", k.sVE, sd)
        for gi in range(4):
            srct = wnat_hi if gi < 2 else wnat_lo
            dstv = WOH if gi < 2 else WOL
            dc0 = (gi % 2) * 8
            srcs = [srct[:, slot * D + (dc0 + j) * 128:
                         slot * D + (dc0 + j + 1) * 128] for j in range(8)]
            tv, ev = k.tgroup(srcs, dstv[:, dc0:dc0 + 8, oc * 128:(oc + 1) * 128],
                              identbf)
            if gi == 3:
                wtrans[oc] = tv
    k.barrier()
    st1.close()

    cxs_hi = k.sb("cxs_hi", [128, 2 * 16 * 256], BF16, stack=st)
    cxs_lo = k.sb("cxs_lo", [128, 2 * 16 * 256], BF16, stack=st)
    CXH = cxs_hi.rearrange("pp (sl dc c) -> pp sl dc c", sl=2, dc=16)
    CXL = cxs_lo.rearrange("pp (sl dc c) -> pp sl dc c", sl=2, dc=16)
    xzb = k.sb("xzb", [128, 2 * D], F32, stack=st)
    ubuf = k.sb("ubuf", [128, D], F32, stack=st)
    lncol = k.sb("lncol", [128, 8], F32, stack=st)
    poutA = k.ps("poutA", [128, 2048], F32, stack=st)
    poutB = k.ps("poutB", [128, 2048], F32, stack=st)
    sCx = [k.dma_sem("sCxa"), k.dma_sem("sCxb")]
    sXc = [k.dma_sem("sXca"), k.dma_sem("sXcb")]
    stU = k.dma_sem("stU")
    cxload = {}
    xload = {}
    mm_done = {}
    z_done = {}
    u_store = {}

    def load_cx(ts):
        slot = ts % 2
        prev = mm_done.get((ts - 2, 1))
        if prev:
            k.wait("sync", k.sTN, prev)
        t0 = ts * 256
        k.load(CXH[:, slot], g["ctx_scr_hi"].rearrange(
            "dc pp t -> pp dc t")[:, :, t0:t0 + 256], sCx[slot])
        cxload[ts] = k.load(CXL[:, slot], g["ctx_scr_lo"].rearrange(
            "dc pp t -> pp dc t")[:, :, t0:t0 + 256], sCx[slot])

    def load_x(ts, tc):
        i = 2 * ts + tc
        slot = i % 2
        prev = u_store.get(i - 2)
        if prev:
            k.wait("sync", stU, prev)
        t0 = ts * 256 + tc * 128
        xload[i] = k.load(xzb[:, slot * D:(slot + 1) * D],
                          g["x"][t0:t0 + 128, :], sXc[slot])

    load_cx(0)
    load_x(0, 0)
    load_x(0, 1)
    for ts in range(8):
        slot = ts % 2
        if ts + 1 < 8:
            load_cx(ts + 1)
        k.wait("tensor", sCx[slot], cxload[ts])
        for tc in range(2):
            i = 2 * ts + tc
            P = poutA if i % 2 == 0 else poutB
            if i >= 2:
                k.wait("tensor", k.sVE, z_done[i - 2])
            tv = None
            for ob in range(4):
                k.op("tensor", lambda e, pp=P[:, ob * 512:(ob + 1) * 512],
                     m=p["bout_row_hi"][:, ob * 512:(ob + 1) * 512]:
                     e.matmul(pp, p["ones_bf"], m, start=True, stop=False,
                              skip_group_check=True))
                k.op("tensor", lambda e, pp=P[:, ob * 512:(ob + 1) * 512],
                     m=p["bout_row_lo"][:, ob * 512:(ob + 1) * 512]:
                     e.matmul(pp, p["ones_bf"], m, start=False, stop=False,
                              skip_group_check=True))
            for dc in range(16):
                for ob in range(4):
                    tv = k.mm3(P[:, ob * 512:(ob + 1) * 512],
                               CXH[:, slot, dc, tc * 128:(tc + 1) * 128],
                               CXL[:, slot, dc, tc * 128:(tc + 1) * 128],
                               WOH[:, dc, ob * 512:(ob + 1) * 512],
                               WOL[:, dc, ob * 512:(ob + 1) * 512],
                               start=False, stop=(dc == 15),
                               inc=k.sTN if (dc == 15 and ob == 3) else None)
            mm_done[(ts, tc)] = tv
            xs = xzb[:, (i % 2) * D:(i % 2 + 1) * D]
            us = ubuf[:, 0:D]
            k.wait("vector", k.sTN, tv)
            k.wait("vector", sXc[i % 2], xload[i])
            zv = k.vec(lambda e, pp=P, x=xs, z=us: e.tensor_tensor(z, pp, x, OP.add))
            z_done[i] = zv
            k.vec(lambda e, z=us: e.reduce_sum(lncol[:, 0:1], z, axis=AX.X))
            k.wait("scalar", k.sVE, zv)
            sq = k.sca(lambda e, z=us, t=xs: e.activation(
                t, z, AF.Square, accum_out=lncol[:, 1:2]))
            k.vec(lambda e: e.tensor_scalar(
                lncol[:, 2:3], lncol[:, 0:1], 1.0 / D, None, OP.mult))
            k.wait("vector", k.sSC, sq)
            k.vec(lambda e: e.tensor_scalar(
                lncol[:, 3:4], lncol[:, 1:2], 1.0 / D, None, OP.mult))
            k.vec(lambda e: e.tensor_tensor(
                lncol[:, 4:5], lncol[:, 2:3], lncol[:, 2:3], OP.mult))
            vv = k.vec(lambda e: e.scalar_tensor_tensor(
                lncol[:, 5:6], lncol[:, 4:5], -1.0, lncol[:, 3:4],
                OP.mult, OP.add))
            vv = k.vec(lambda e: e.tensor_scalar(
                lncol[:, 5:6], lncol[:, 5:6], EPS, None, OP.add))
            k.wait("scalar", k.sVE, vv)
            sq2 = k.sca(lambda e: e.activation(
                lncol[:, 6:7], lncol[:, 5:6], AF.Sqrt))
            k.wait("vector", k.sSC, sq2)
            k.vec(lambda e: e.reciprocal(lncol[:, 7:8], lncol[:, 6:7]))
            uv = k.vec(lambda e, z=us, u=xs: e.tensor_scalar(
                u, z, lncol[:, 2:3], lncol[:, 7:8], OP.subtract, OP.mult))
            k.wait("gpsimd", k.sVE, uv)
            t0 = ts * 256 + tc * 128
            u_store[i] = k.store(g["u_scr"][t0:t0 + 128, :], xs, stU)
            if i + 2 < 16:
                load_x((i + 2) // 2, (i + 2) % 2)
    st.close()


def phase4(k, g, p, b1_off):
    st = contextlib.ExitStack()
    identbf = p["identbf"]
    ident32 = p["ident32"]
    u_tm = k.sb("u_tm", [128, 4 * D], F32, stack=st)
    g_hi = k.sb("g_hi", [128, 8 * 512], BF16, stack=st)
    g_lo = k.sb("g_lo", [128, 8 * 512], BF16, stack=st)
    acc = k.sb("acc", [128, 8 * 512], F32, stack=st)
    wsf = k.sb("wsf", [128, D], F32, stack=st)
    wsh = k.sb("wsh", [128, 2 * D], BF16, stack=st)
    wsl = k.sb("wsl", [128, 2 * D], BF16, stack=st)
    bcw = k.sb("bcw", [128, 512], F32, stack=st)
    gtmp = k.sb("gtmp", [128, 512], F32, stack=st)
    wext = k.sb("wext", [9, 512], F32, stack=st)
    wext_hi = k.sb("wext_hi", [9, 512], BF16, stack=st)
    wext_lo = k.sb("wext_lo", [9, 512], BF16, stack=st)
    lgt = k.sb("lgt", [128, 8], F32, stack=st)
    prb = k.sb("prb", [128, 8], F32, stack=st)
    prb2 = k.sb("prb2", [128, 8], F32, stack=st)
    mk = k.sb("mk", [128, 8], F32, stack=st)
    mk1 = k.sb("mk1", [128, 8], F32, stack=st)
    mk2 = k.sb("mk2", [128, 8], F32, stack=st)
    sc8 = k.sb("sc8", [128, 8], F32, stack=st)
    tmp8 = k.sb("tmp8", [128, 8], F32, stack=st)
    wt8 = k.sb("wt8", [128, 8], F32, stack=st)
    wt8b = k.sb("wt8b", [128, 8], F32, stack=st)
    cc = k.sb("cc", [128, 8], F32, stack=st)
    z2b = wsf[:, 0:D]
    PS1 = k.ps("PS1", [128, 2048], F32, stack=st)
    PS2 = k.ps("PS2", [128, 2048], F32, stack=st)
    GV_H = g_hi.rearrange("pp (kc c) -> pp kc c", kc=8)
    GV_L = g_lo.rearrange("pp (kc c) -> pp kc c", kc=8)
    ACCV = acc.rearrange("pp (kc c) -> pp kc c", kc=8)
    UV = u_tm.rearrange("pp (tc c) -> pp tc c", tc=4)

    k.ptr = PS2[:, 0:1024].bitcast(BF16)
    k.ptr_view = k.ptr.rearrange("pp (bk j c) -> pp bk j c", bk=2, j=8)
    ptr32 = PS2[:, 0:1024]
    ptr32_view = ptr32.rearrange("pp (bk j c) -> pp bk j c", bk=2, j=4)

    def tgroup32(srcs, dst):
        bank = k.tg_bank
        k.tg_bank ^= 1
        n = len(srcs)
        k.wait("tensor", k.sVE, k.ptr_last[bank])
        tv = None
        for j, s in enumerate(srcs):
            tv = k.op("tensor",
                      lambda e, pp=ptr32[:, bank * 512 + j * 128:
                                         bank * 512 + (j + 1) * 128],
                      s2=s: e.transpose(pp, s2, ident32),
                      inc=k.sTN if j == n - 1 else None)
        k.wait("vector", k.sTN, tv)
        ev = k.vec(lambda e, d=dst, s=ptr32_view[:, bank, 0:n, :]:
                   e.tensor_copy(d, s))
        k.ptr_last[bank] = ev
        return tv, ev

    sU = [k.dma_sem("sUa"), k.dma_sem("sUb")]
    sWs = [k.dma_sem("sWsa"), k.dma_sem("sWsb")]
    sOne = k.dma_sem("sOne")
    k.load(wext[8:9, :], g["c_ones512"][:, :], sOne)
    stOut = k.dma_sem("stOut")
    k.out_sem = stOut

    class WS:
        def __init__(self):
            self.slot = 0
            self.cons = [0, 0]

        def feed(self, hi_src, lo_src, width):
            sl = self.slot
            self.slot ^= 1
            if self.cons[sl]:
                k.wait("sync", k.sTN, self.cons[sl])
            k.load(wsh[:, sl * D:sl * D + width], hi_src, sWs[sl])
            lv = k.load(wsl[:, sl * D:sl * D + width], lo_src, sWs[sl])
            hi = wsh[:, sl * D:sl * D + width].rearrange(
                "pp (kc c) -> pp kc c", c=128)
            lo = wsl[:, sl * D:sl * D + width].rearrange(
                "pp (kc c) -> pp kc c", c=128)
            return sl, hi, lo, lv

        def done(self, sl, tv):
            self.cons[sl] = tv

    ws = WS()
    w1base4 = []
    o1 = 0
    for he in HIDS:
        w1base4.append(o1)
        o1 += he // 128
    w2base4 = []
    o2 = 0
    for he in HIDS:
        w2base4.append(o2)
        o2 += he // 256
    bank5_evac = [0]
    t2_evac = [0, 0]
    ps1_evac = [0, 0, 0, 0]
    uload = {}
    out_store = {}
    last_final_tv = [0]

    ovl = k.sb("ovl", [128, 40960], BF16, stack=st)
    for ts in range(4):
        k.barrier()
        t0s = ts * 512
        u_fm_hi = ovl[:, 0:8192]
        u_fm_lo = ovl[:, 8192:16384]
        UFH = u_fm_hi.rearrange("pp (kc c) -> pp kc c", kc=16)
        UFL = u_fm_lo.rearrange("pp (kc c) -> pp kc c", kc=16)
        usplit = {}
        for tc in range(4):
            lv = k.load(UV[:, tc, :], g["u_scr"][t0s + tc * 128:
                                                 t0s + (tc + 1) * 128, :],
                        sU[tc % 2])
            uload[tc] = lv
        for tc in range(4):
            k.wait("vector", sU[tc % 2], uload[tc])
            sd = k.split(wsh[:, 0:D], wsl[:, 0:D], UV[:, tc, :])
            k.wait("tensor", k.sVE, sd)
            for gi in range(4):
                srct = wsh if gi < 2 else wsl
                dstv = UFH if gi < 2 else UFL
                kc0 = (gi % 2) * 8
                srcs = [srct[:, (kc0 + j) * 128:(kc0 + j + 1) * 128]
                        for j in range(8)]
                tv, ev = k.tgroup(srcs, dstv[:, kc0:kc0 + 8,
                                             tc * 128:(tc + 1) * 128], identbf)
            usplit[tc] = tv
        ws.cons[0] = max(ws.cons[0], tv)
        ws.cons[1] = max(ws.cons[1], tv)
        for tc in range(4):
            lg = PS2[:, 1024:1032]
            k.wait("tensor", k.sVE, t2_evac[0])
            k.wait("tensor", k.sVE, bank5_evac[0])
            k.op("tensor", lambda e, m=p["rb_eff_hi"]: e.matmul(
                lg, p["ones_bf"], m, start=True, stop=False,
                skip_group_check=True))
            k.op("tensor", lambda e, m=p["rb_eff_lo"]: e.matmul(
                lg, p["ones_bf"], m, start=False, stop=False,
                skip_group_check=True))
            tv = None
            for kc in range(16):
                tv = k.mm3(lg, UFH[:, kc, tc * 128:(tc + 1) * 128],
                           UFL[:, kc, tc * 128:(tc + 1) * 128],
                           p["rw_hi"][:, kc * 8:(kc + 1) * 8],
                           p["rw_lo"][:, kc * 8:(kc + 1) * 8],
                           start=False, stop=(kc == 15),
                           inc=k.sTN if kc == 15 else None)
            k.wait("vector", k.sTN, tv)
            k.vec(lambda e: e.tensor_copy(lgt[:, :], lg))
            k.vec(lambda e: e.reduce_max(cc[:, 0:1], lgt[:, :], axis=AX.X))
            nm = k.vec(lambda e: e.tensor_scalar(
                cc[:, 1:2], cc[:, 0:1], -1.0, None, OP.mult))
            k.wait("scalar", k.sVE, nm)
            sv = k.sca(lambda e: e.activation(
                prb[:, :], lgt[:, :], AF.Exp, bias=cc[:, 1:2],
                accum_out=cc[:, 2:3]))
            t2_evac[0] = max(t2_evac[0], nm)
            k.wait("vector", k.sSC, sv)
            k.vec(lambda e: e.reciprocal(cc[:, 3:4], cc[:, 2:3]))
            k.vec(lambda e: e.tensor_scalar(
                prb2[:, :], prb[:, :], cc[:, 3:4], None, OP.mult))
            k.vec(lambda e: e.reduce_max(cc[:, 0:1], prb2[:, :], axis=AX.X))
            k.vec(lambda e: e.tensor_scalar(
                mk[:, :], prb2[:, :], cc[:, 0:1], None, OP.is_equal))
            k.vec(lambda e: e.tensor_tensor(
                sc8[:, :], mk[:, :], p["rank8"][:, :], OP.mult))
            k.vec(lambda e: e.reduce_max(cc[:, 1:2], sc8[:, :], axis=AX.X))
            k.vec(lambda e: e.tensor_scalar(
                mk1[:, :], sc8[:, :], cc[:, 1:2], None, OP.is_equal))
            k.vec(lambda e: e.tensor_tensor(
                tmp8[:, :], prb2[:, :], mk1[:, :], OP.mult))
            k.vec(lambda e: e.reduce_sum(cc[:, 4:5], tmp8[:, :], axis=AX.X))
            k.vec(lambda e: e.scalar_tensor_tensor(
                sc8[:, :], mk1[:, :], -1e9, prb2[:, :], OP.mult, OP.add))
            k.vec(lambda e: e.reduce_max(cc[:, 0:1], sc8[:, :], axis=AX.X))
            k.vec(lambda e: e.tensor_scalar(
                mk[:, :], sc8[:, :], cc[:, 0:1], None, OP.is_equal))
            k.vec(lambda e: e.tensor_tensor(
                sc8[:, :], mk[:, :], p["rank8"][:, :], OP.mult))
            k.vec(lambda e: e.reduce_max(cc[:, 1:2], sc8[:, :], axis=AX.X))
            k.vec(lambda e: e.tensor_scalar(
                mk2[:, :], sc8[:, :], cc[:, 1:2], None, OP.is_equal))
            k.vec(lambda e: e.tensor_tensor(
                tmp8[:, :], prb2[:, :], mk2[:, :], OP.mult))
            k.vec(lambda e: e.reduce_sum(cc[:, 5:6], tmp8[:, :], axis=AX.X))
            dv = k.vec(lambda e: e.tensor_tensor(
                cc[:, 6:7], cc[:, 5:6], cc[:, 4:5], OP.subtract))
            k.wait("scalar", k.sVE, dv)
            sv = k.sca(lambda e: e.activation(cc[:, 7:8], cc[:, 6:7], AF.Exp))
            k.wait("vector", k.sSC, sv)
            k.vec(lambda e: e.tensor_scalar(
                cc[:, 0:1], cc[:, 7:8], 1.0, None, OP.add))
            k.vec(lambda e: e.reciprocal(cc[:, 1:2], cc[:, 0:1]))
            k.vec(lambda e: e.tensor_tensor(
                cc[:, 2:3], cc[:, 7:8], cc[:, 1:2], OP.mult))
            k.vec(lambda e: e.tensor_scalar(
                wt8[:, :], mk1[:, :], cc[:, 1:2], None, OP.mult))
            wv = k.vec(lambda e: e.scalar_tensor_tensor(
                wt8b[:, :], mk2[:, :], cc[:, 2:3], wt8[:, :], OP.mult, OP.add))
            k.wait("tensor", k.sVE, wv)
            k.wait("tensor", k.sVE, bank5_evac[0])
            ttv = k.op("tensor", lambda e: e.transpose(
                PS2[0:8, 1536:1664], wt8b[:, :], ident32), inc=k.sTN)
            k.wait("vector", k.sTN, ttv)
            bv = k.vec(lambda e, tc=tc: e.tensor_copy(
                wext[0:8, tc * 128:(tc + 1) * 128], PS2[0:8, 1536:1664]))
            t2_evac[1] = max(t2_evac[1], bv)
        k.wait("vector", sOne, 16)
        swv = k.split(wext_hi[:, :], wext_lo[:, :], wext[:, :])

        for oc in range(8):
            sl, whi, wlo, sd = ws.feed(g["sc_down_hi"][oc],
                                       g["sc_down_lo"][oc], 2048)
            k.wait("tensor", sWs[sl], sd)
            bank = oc % 2
            pt2 = PS2[:, 1024 + bank * 512:1024 + (bank + 1) * 512]
            k.wait("tensor", k.sVE, t2_evac[bank])
            tv = None
            for kc in range(16):
                tv = k.mm3(pt2, whi[:, kc, :], wlo[:, kc, :],
                           UFH[:, kc, :], UFL[:, kc, :],
                           start=(kc == 0), stop=(kc == 15),
                           inc=k.sTN if kc == 15 else None)
            ws.done(sl, tv)
            k.wait("scalar", k.sTN, tv)
            k.wait("scalar", k.sVE, t2_evac[1 - bank])
            gv = k.sca(lambda e, pp=pt2, oc=oc: e.activation(
                gtmp[:, :], pp, AF.Gelu, bias=p["db_cols"][:, oc:oc + 1]))
            k.wait("vector", k.sSC, gv)
            ev = k.split(GV_H[:, oc, :], GV_L[:, oc, :], gtmp[:, :])
            t2_evac[bank] = ev
        t3p_hi = ovl[:, 0:16384]
        t3p_lo = ovl[:, 16384:32768]
        T3H = t3p_hi.rearrange("pp (kc c) -> pp kc c", kc=32)
        T3L = t3p_lo.rearrange("pp (kc c) -> pp kc c", kc=32)

        k.wait("vector", k.sVE, swv)
        bank5_evac[0] = max(bank5_evac[0], k.ptr_last[0], k.ptr_last[1])
        for ocm in range(8):
            k.wait("tensor", k.sVE, bank5_evac[0])
            pb5 = PS2[:, 512:1024]
            tv = k.mm3(pb5, p["bmat_hi"][:, ocm * 128:(ocm + 1) * 128],
                       p["bmat_lo"][:, ocm * 128:(ocm + 1) * 128],
                       wext_hi[:, :], wext_lo[:, :],
                       start=True, stop=True, inc=k.sTN)
            k.wait("vector", k.sTN, tv)
            bank5_evac[0] = k.vec(lambda e, ocm=ocm: e.tensor_copy(
                ACCV[:, ocm, :], pb5))
            t2_evac[1] = bank5_evac[0]

        for ei in range(9):
            shared = (ei == 8)
            he = LHID if shared else HIDS[ei]
            kcs = he // 128
            w1name = "shared_w1" if shared else f"e{ei}_w1"
            w2name = "shared_w2" if shared else f"e{ei}_w2"
            bcol = p["sb1_cols"] if shared else p["b1_cols"]
            boff = 0 if shared else b1_off[ei]
            if not shared:
                k.wait("tensor", k.sVE, bank5_evac[0])
                pb5 = PS2[:, 512:1024]
                k.op("tensor", lambda e, ei=ei: e.matmul(
                    pb5, p["sel8"][:, ei * 128:(ei + 1) * 128], wext_hi[:, :],
                    start=True, stop=False, skip_group_check=True))
                tv = k.op("tensor", lambda e, ei=ei: e.matmul(
                    pb5, p["sel8"][:, ei * 128:(ei + 1) * 128], wext_lo[:, :],
                    start=False, stop=True, skip_group_check=True), inc=k.sTN)
                k.wait("vector", k.sTN, tv)
                bank5_evac[0] = k.vec(lambda e: e.tensor_copy(bcw[:, :], pb5))
            w1b = (160 if shared else w1base4[ei])
            for oc in range(kcs):
                sl, whi, wlo, sd = ws.feed(g["sc_w1_hi"][w1b + oc],
                                           g["sc_w1_lo"][w1b + oc], 1024)
                k.wait("tensor", sWs[sl], sd)
                bank = oc % 2
                pt2 = PS2[:, 1024 + bank * 512:1024 + (bank + 1) * 512]
                k.wait("tensor", k.sVE, t2_evac[bank])
                tv = None
                for kc in range(8):
                    tv = k.mm3(pt2, whi[:, kc, :], wlo[:, kc, :],
                               GV_H[:, kc, :], GV_L[:, kc, :],
                               start=(kc == 0), stop=(kc == 7),
                               inc=k.sTN if kc == 7 else None)
                ws.done(sl, tv)
                k.wait("scalar", k.sTN, tv)
                k.wait("scalar", k.sVE, t2_evac[1 - bank])
                gv = k.sca(lambda e, pp=pt2, c=bcol[:, boff + oc:boff + oc + 1]:
                           e.activation(gtmp[:, :], pp, AF.Gelu, bias=c))
                k.wait("vector", k.sSC, gv)
                if shared:
                    qv = k.vec(lambda e: e.tensor_scalar(
                        gtmp[:, :], gtmp[:, :], 0.1, None, OP.mult))
                else:
                    qv = k.vec(lambda e: e.tensor_tensor(
                        gtmp[:, :], gtmp[:, :], bcw[:, :], OP.mult))
                ev = k.split(T3H[:, oc, :], T3L[:, oc, :], gtmp[:, :])
                t2_evac[bank] = ev
            w2b = (80 if shared else w2base4[ei])
            k.wait("tensor", k.sVE, k.sVE.n)
            for grp in range(2):
                tv = None
                for kc2 in range(kcs // 2):
                    sl, whi, wlo, sd = ws.feed(g["sc_w2_hi"][w2b + kc2],
                                               g["sc_w2_lo"][w2b + kc2], 2048)
                    k.wait("tensor", sWs[sl], sd)
                    if kc2 == 0:
                        for j in range(4):
                            k.wait("tensor", k.sVE, ps1_evac[j])
                    for hf in range(2):
                        kc = 2 * kc2 + hf
                        for j in range(4):
                            oc2 = grp * 4 + j
                            pmm = PS1[:, j * 512:(j + 1) * 512]
                            tv = k.mm3(pmm, whi[:, hf * 8 + oc2, :],
                                       wlo[:, hf * 8 + oc2, :],
                                       T3H[:, kc, :], T3L[:, kc, :],
                                       start=(kc == 0),
                                       stop=(kc == kcs - 1),
                                       inc=k.sTN if (j == 3 and hf == 1)
                                       else None)
                    ws.done(sl, tv)
                k.wait("vector", k.sTN, tv)
                for j in range(4):
                    oc2 = grp * 4 + j
                    ev = k.vec(lambda e, oc2=oc2, j=j: e.tensor_tensor(
                        ACCV[:, oc2, :], ACCV[:, oc2, :],
                        PS1[:, j * 512:(j + 1) * 512], OP.add))
                    ps1_evac[j] = ev

        k.ptr_last[1] = max(k.ptr_last[1], bank5_evac[0])
        gu_hi = ovl[:, 0:8192]
        gu_lo = ovl[:, 8192:16384]
        ffn_tm = ovl[:, 16384:32768].bitcast(F32)
        acc_hi = ovl[:, 32768:36864]
        acc_lo = ovl[:, 36864:40960]
        GUH = gu_hi.rearrange("pp (kc c) -> pp kc c", kc=16)
        GUL = gu_lo.rearrange("pp (kc c) -> pp kc c", kc=16)
        AHV = acc_hi.rearrange("pp (kc c) -> pp kc c", kc=8)
        ALV = acc_lo.rearrange("pp (kc c) -> pp kc c", kc=8)
        FFV = ffn_tm.rearrange("pp (tc c) -> pp tc c", tc=4)
        k.wait("vector", k.sTN, k.sTN.n)
        sda = k.split(acc_hi[:, :], acc_lo[:, :], acc[:, :])
        k.wait("tensor", k.sVE, sda)
        gtf = wsf[:, 0:1024]
        gth = wsh[:, 0:1024]
        gtl = wsl[:, 0:1024]
        gusd = k.sVE.n
        for tc in range(4):
            for half in range(2):
                k.wait("scalar", k.sVE, gusd)
                sv = k.sca(lambda e, tc=tc, half=half: e.activation(
                    gtf, UV[:, tc, half * 1024:(half + 1) * 1024],
                    AF.Gelu))
                k.wait("vector", k.sSC, sv)
                sd = k.split(gth, gtl, gtf)
                gusd = sd
                k.wait("tensor", k.sVE, sd)
                kc0 = half * 8
                for gi in range(2):
                    srct = gth if gi == 0 else gtl
                    dstv = GUH if gi == 0 else GUL
                    srcs = [srct[:, j * 128:(j + 1) * 128] for j in range(8)]
                    tv, ev = k.tgroup(srcs, dstv[:, kc0:kc0 + 8,
                                                 tc * 128:(tc + 1) * 128],
                                      identbf)
        ws.cons[0] = max(ws.cons[0], tv)
        ws.cons[1] = max(ws.cons[1], tv)
        k.wait("tensor", k.sVE, k.sVE.n)
        fin_evac = [0, 0, 0, 0]
        for oc in range(16):
            slu, uhi, ulo, sdu = ws.feed(g["sc_up_hi"][oc],
                                         g["sc_up_lo"][oc], 1024)
            slc, chi, clo, sdc = ws.feed(g["sc_core_hi"][oc],
                                         g["sc_core_lo"][oc], 2048)
            k.wait("tensor", sWs[slu], sdu)
            k.wait("tensor", sWs[slc], sdc)
            bank = oc % 4
            pf = PS1[:, bank * 512:(bank + 1) * 512]
            k.wait("tensor", k.sVE, fin_evac[bank])
            tv = None
            for kc in range(8):
                k.mm3(pf, uhi[:, kc, :], ulo[:, kc, :],
                      AHV[:, kc, :], ALV[:, kc, :],
                      start=(kc == 0), stop=False)
            for kc in range(16):
                tv = k.mm3(pf, chi[:, kc, :], clo[:, kc, :],
                           GUH[:, kc, :], GUL[:, kc, :],
                           start=False, stop=(kc == 15),
                           inc=k.sTN if kc == 15 else None)
            ws.done(slu, tv)
            ws.done(slc, tv)
            last_final_tv[0] = tv
            k.wait("vector", k.sTN, k.sTN.n)
            fv = k.vec(lambda e, pp=pf, oc=oc: e.tensor_scalar(
                bcw[:, 0:512], pp, p["ubcb_cols"][:, oc:oc + 1], None, OP.add))
            fin_evac[bank] = fv
            k.wait("tensor", k.sVE, fv)
            srcs = [bcw[:, j * 128:(j + 1) * 128] for j in range(4)]
            tcs = [FFV[:, j, oc * 128:(oc + 1) * 128] for j in range(4)]
            dst = ffn_tm.rearrange("pp (tc c) -> pp tc c", tc=4)[
                :, :, oc * 128:(oc + 1) * 128]
            tv2, ev2 = tgroup32(srcs, dst)
            fin_evac[bank] = max(fin_evac[bank], ev2)
        for tc in range(4):
            k.wait("vector", k.sVE, k.sVE.n)
            zv = k.vec(lambda e, tc=tc: e.tensor_tensor(
                z2b[:, :], FFV[:, tc, :], UV[:, tc, :], OP.add))
            k.vec(lambda e: e.reduce_sum(cc[:, 0:1], z2b[:, :], axis=AX.X))
            k.wait("scalar", k.sVE, zv + 1)
            sq = k.sca(lambda e, tc=tc: e.activation(
                UV[:, tc, 0:D], z2b[:, :], AF.Square, accum_out=cc[:, 1:2]))
            k.vec(lambda e: e.tensor_scalar(
                cc[:, 2:3], cc[:, 0:1], 1.0 / D, None, OP.mult))
            k.wait("vector", k.sSC, sq)
            k.vec(lambda e: e.tensor_scalar(
                cc[:, 3:4], cc[:, 1:2], 1.0 / D, None, OP.mult))
            k.vec(lambda e: e.tensor_tensor(
                cc[:, 4:5], cc[:, 2:3], cc[:, 2:3], OP.mult))
            k.vec(lambda e: e.scalar_tensor_tensor(
                cc[:, 5:6], cc[:, 4:5], -1.0, cc[:, 3:4], OP.mult, OP.add))
            vv = k.vec(lambda e: e.tensor_scalar(
                cc[:, 5:6], cc[:, 5:6], EPS, None, OP.add))
            k.wait("scalar", k.sVE, vv)
            sq2 = k.sca(lambda e: e.activation(cc[:, 6:7], cc[:, 5:6], AF.Sqrt))
            k.wait("vector", k.sSC, sq2)
            k.vec(lambda e: e.reciprocal(cc[:, 7:8], cc[:, 6:7]))
            uv = k.vec(lambda e, tc=tc: e.tensor_scalar(
                UV[:, tc, 0:D], z2b[:, :], cc[:, 2:3], cc[:, 7:8],
                OP.subtract, OP.mult))
            k.wait("gpsimd", k.sVE, uv)
            t0 = t0s + tc * 128
            out_store[(ts, tc)] = k.store(g["out"][t0:t0 + 128, :],
                                          UV[:, tc, 0:D], stOut)
    st.close()


def make_consts():
    import ml_dtypes
    c = {
        "c_identbf": np.eye(128, dtype=ml_dtypes.bfloat16),
        "c_ident32": np.eye(128, dtype=np.float32),
        "c_rank8": np.broadcast_to(
            np.arange(8, 0, -1, dtype=np.float32)[None, :], (128, 8)).copy(),
    }
    lam64 = np.float64(COST_LAMBDA)
    cost64 = np.array([2048.0 * h for h in HIDS], dtype=np.float64)
    v64 = -(lam64 * cost64)
    hi = v64.astype(np.float32)
    lo = (v64 - hi.astype(np.float64)).astype(np.float32)
    c["c_ones_bf"] = np.ones((1, 128), dtype=ml_dtypes.bfloat16)
    sel = np.zeros((9, 8 * 128), dtype=ml_dtypes.bfloat16)
    for e in range(8):
        sel[e, e * 128:(e + 1) * 128] = 1
    c["c_sel"] = sel
    c["c_ones512"] = np.ones((1, 512), dtype=np.float32)
    c["c_neglc_hi"] = hi.reshape(1, 8)
    c["c_neglc_lo"] = lo.reshape(1, 8)
    return c


def make_in_maps(inputs):
    x_full = np.ascontiguousarray(np.asarray(inputs["L0_attn_hidden_states"],
                                             dtype=np.float32))

    def row(v):
        return np.ascontiguousarray(np.asarray(v, dtype=np.float32)).reshape(1, -1)

    def mat(v):
        return np.ascontiguousarray(np.asarray(v, dtype=np.float32))

    base = {
        "w_qkv": mat(inputs["L0_attn_w_qkv"]), "b_qkv": row(inputs["L0_attn_b_qkv"]),
        "w_out": mat(inputs["L0_attn_w_out"]), "b_out": row(inputs["L0_attn_b_out"]),
        "down_w": mat(inputs["L0_ffn_down_w"]), "down_b": row(inputs["L0_ffn_down_b"]),
        "up_w": mat(inputs["L0_ffn_up_w"]), "up_b": row(inputs["L0_ffn_up_b"]),
        "router_w": mat(inputs["L0_ffn_router_w"]),
        "router_b": row(inputs["L0_ffn_router_b"]),
        "shared_w1": mat(inputs["L0_ffn_shared_w1"]),
        "shared_b1": row(inputs["L0_ffn_shared_b1"]),
        "shared_w2": mat(inputs["L0_ffn_shared_w2"]),
        "shared_b2": row(inputs["L0_ffn_shared_b2"]),
        "core_w": mat(inputs["L0_ffn_core_w"]), "core_b": row(inputs["L0_ffn_core_b"]),
    }
    for i in range(8):
        base[f"e{i}_w1"] = mat(inputs[f"L0_ffn_e{i}_w1"])
        base[f"e{i}_b1"] = row(inputs[f"L0_ffn_e{i}_b1"])
        base[f"e{i}_w2"] = mat(inputs[f"L0_ffn_e{i}_w2"])
        base[f"e{i}_b2"] = row(inputs[f"L0_ffn_e{i}_b2"])
    base.update(make_consts())
    return [dict(base, x=np.ascontiguousarray(x_full[c])) for c in range(NCORE)]


def kernel(**inputs):
    in_maps = make_in_maps(inputs)
    nc = build_program()
    res = run_bass_kernel_spmd(nc, in_maps, core_ids=list(range(NCORE)))
    out = np.stack([np.asarray(r["out"]) for r in res.results], axis=0)
    return out.astype(np.float32)


# version 06
# speedup vs baseline: 1.1299x; 1.1299x over previous
import contextlib
import numpy as np
import concourse.bass as bass
import concourse.mybir as mybir
from concourse.bass_utils import run_bass_kernel_spmd

F32 = mybir.dt.float32
BF16 = mybir.dt.bfloat16
AF = mybir.ActivationFunctionType
OP = mybir.AluOpType
AX = mybir.AxisListType

D = 2048
T = 2048
NCORE = 8
HIDS = [1024, 2048, 3072, 4096, 1024, 2048, 3072, 4096]
LHID = 1024
COST_LAMBDA = 1e-7
EPS = 1e-5
INV_S64 = float(np.float64(1.0) / np.sqrt(np.float64(2048.0)))
INV_S_HI = float(np.float32(INV_S64))
INV_S_LO = float(np.float32(INV_S64 - np.float64(np.float32(INV_S64))))

ENGINES = ("tensor", "vector", "scalar", "sync", "gpsimd")

DEBUG_TAPS = ()
BUILD_PHASES = 99


class Sem:
    __slots__ = ("h", "n")

    def __init__(self, h):
        self.h = h
        self.n = 0

    def inc(self, k=1):
        self.n += k
        return self.n


class K:

    def __init__(self):
        self.nc = bass.Bass("TRN2", target_bir_lowering=False)
        self.q = {e: [] for e in ENGINES}
        self.es = contextlib.ExitStack()
        self.dma_sems = []
        self.sTN = Sem(self.nc.alloc_semaphore("sTN"))
        self.sVE = Sem(self.nc.alloc_semaphore("sVE"))
        self.sSC = Sem(self.nc.alloc_semaphore("sSC"))
        self.ptr_last = [0, 0]
        self.ptr = None
        self.ptr_view = None
        self.tg_bank = 0
        self.dirty = {e: False for e in ENGINES}
        self.fencebuf = self.sb("fencebuf", [1, 8], F32)
        self.op("vector", lambda e: e.memset(self.fencebuf[:, :], 0.0),
                inc=self.sVE)

    def sb(self, name, shape, dtype, stack=None):
        g = self.nc.sbuf_tensor(name, list(shape), dtype)
        h = (stack or self.es).enter_context(g)
        return h.ap()

    def ps(self, name, shape, dtype=F32, stack=None):
        g = self.nc.psum_tensor(name, list(shape), dtype)
        h = (stack or self.es).enter_context(g)
        return h.ap()

    def emit(self, eng, fn):
        self.q[eng].append(fn)

    def wait(self, eng, sem, val):
        if val and val > 0:
            self.emit(eng, lambda e, s=sem.h, v=val: e.wait_ge(s, v))

    def op(self, eng, fn, inc=None, inc_k=1):
        if inc is None:
            self.dirty[eng] = True
            self.emit(eng, fn)
            return None
        self.dirty[eng] = False
        v = inc.inc(inc_k)
        self.emit(eng, lambda e, f=fn, s=inc.h, k=inc_k: f(e).then_inc(s, k))
        return v

    def fence(self, eng):
        sem = {"vector": self.sVE, "scalar": self.sSC}[eng]
        if eng == "vector":
            return self.op(eng, lambda e: e.tensor_copy(
                self.fencebuf[0:1, 2:3], self.fencebuf[0:1, 0:1]), inc=sem)
        return self.op(eng, lambda e: e.copy(
            self.fencebuf[0:1, 3:4], self.fencebuf[0:1, 0:1]), inc=sem)

    def barrier(self):
        assert not self.dirty["tensor"], \
            "barrier: tensor stream must end with an inc-carrying instruction"
        for e in ("vector", "scalar"):
            if self.dirty[e]:
                self.fence(e)
        tt, vt, sct = self.sTN.n, self.sVE.n, self.sSC.n
        dw = [(s, s.n) for s in self.dma_sems if s.n > 0]
        for e in ENGINES:
            self.wait(e, self.sTN, tt)
            self.wait(e, self.sVE, vt)
            self.wait(e, self.sSC, sct)
            for s, v in dw:
                self.wait(e, s, v)

    def dma_sem(self, name):
        s = Sem(self.nc.alloc_semaphore(name))
        self.dma_sems.append(s)
        return s

    def load(self, dst, src, sem, queue="sync"):
        self.wait(queue, sem, sem.n)
        return self.op(queue, lambda e, d=dst, s=src: e.dma_start(out=d, in_=s),
                       inc=sem, inc_k=16)

    def store(self, dst, src, sem):
        self.wait("gpsimd", sem, sem.n)
        return self.op("gpsimd", lambda e, d=dst, s=src: e.dma_start(out=d, in_=s),
                       inc=sem, inc_k=16)

    def vec(self, fn, inc=None):
        self.wait("vector", self.sVE, self.sVE.n)
        return self.op("vector", fn, inc=inc or self.sVE)

    def sca(self, fn, inc=None):
        self.wait("scalar", self.sSC, self.sSC.n)
        return self.op("scalar", fn, inc=inc or self.sSC)

    def split(self, hi, lo, src, inc=None):
        self.vec(lambda e, d=hi, s=src: e.tensor_copy(d, s))
        return self.vec(lambda e, d=lo, s=src, h=hi:
                        e.tensor_tensor(d, s, h, OP.subtract), inc=inc)

    def mm3(self, psum, whi, wlo, xhi, xlo, start, stop, inc=None):
        self.op("tensor", lambda e, p=psum, w=whi, x=xhi, st=start:
                e.matmul(p, w, x, start=st, stop=False, skip_group_check=True))
        self.op("tensor", lambda e, p=psum, w=whi, x=xlo:
                e.matmul(p, w, x, start=False, stop=False, skip_group_check=True))
        return self.op("tensor", lambda e, p=psum, w=wlo, x=xhi, sp=stop:
                       e.matmul(p, w, x, start=False, stop=sp,
                                skip_group_check=True), inc=inc)

    def tgroup(self, srcs, dst, ident):
        bank = self.tg_bank
        self.tg_bank ^= 1
        n = len(srcs)
        self.wait("tensor", self.sVE, self.ptr_last[bank])
        tv = None
        for j, s in enumerate(srcs):
            tv = self.op("tensor",
                         lambda e, p=self.ptr[:, bank * 1024 + j * 128:
                                              bank * 1024 + (j + 1) * 128],
                         s2=s, i=ident: e.transpose(p, s2, i),
                         inc=self.sTN if j == n - 1 else None)
        self.wait("vector", self.sTN, tv)
        ev = self.vec(lambda e, d=dst,
                      s=self.ptr_view[:, bank, 0:n, :]: e.tensor_copy(d, s))
        self.ptr_last[bank] = ev
        return tv, ev


def build_program():
    k = K()
    nc = k.nc
    dr = {}

    def din(name, shape, dtype=F32):
        dr[name] = nc.dram_tensor(name, list(shape), dtype, kind="ExternalInput").ap()
        return dr[name]

    g = dict(
        x=din("x", [T, D]),
        w_qkv=din("w_qkv", [3 * D, D]), b_qkv=din("b_qkv", [1, 3 * D]),
        w_out=din("w_out", [D, D]), b_out=din("b_out", [1, D]),
        down_w=din("down_w", [D, LHID]), down_b=din("down_b", [1, LHID]),
        up_w=din("up_w", [LHID, D]), up_b=din("up_b", [1, D]),
        router_w=din("router_w", [D, 8]), router_b=din("router_b", [1, 8]),
        shared_w1=din("shared_w1", [LHID, LHID]), shared_b1=din("shared_b1", [1, LHID]),
        shared_w2=din("shared_w2", [LHID, LHID]), shared_b2=din("shared_b2", [1, LHID]),
        core_w=din("core_w", [D, D]), core_b=din("core_b", [1, D]),
        c_identbf=din("c_identbf", [128, 128], BF16),
        c_ident32=din("c_ident32", [128, 128]),
        c_ones_bf=din("c_ones_bf", [1, 128], BF16),
        c_sel=din("c_sel", [9, 8 * 128], BF16),
        c_ones512=din("c_ones512", [1, 512]),
        c_rank8=din("c_rank8", [128, 8]),
        c_neglc_hi=din("c_neglc_hi", [1, 8]),
        c_neglc_lo=din("c_neglc_lo", [1, 8]),
    )
    for i, he in enumerate(HIDS):
        g[f"e{i}_w1"] = din(f"e{i}_w1", [LHID, he])
        g[f"e{i}_b1"] = din(f"e{i}_b1", [1, he])
        g[f"e{i}_w2"] = din(f"e{i}_w2", [he, LHID])
        g[f"e{i}_b2"] = din(f"e{i}_b2", [1, LHID])
    g["out"] = nc.dram_tensor("out", [T, D], F32, kind="ExternalOutput").ap()

    def scr(name, shape, dtype):
        kind = "ExternalOutput" if name in DEBUG_TAPS else "Internal"
        g[name] = nc.dram_tensor(name, list(shape), dtype, kind=kind).ap()
        return g[name]

    scr("q_scr_hi", [16, 128, T], BF16)
    scr("q_scr_lo", [16, 128, T], BF16)
    scr("k_scr_hi", [16, 128, T], BF16)
    scr("k_scr_lo", [16, 128, T], BF16)
    scr("v_scr_hi", [16, T, 128], BF16)
    scr("v_scr_lo", [16, T, 128], BF16)
    scr("ctx_scr_hi", [16, 128, T], BF16)
    scr("ctx_scr_lo", [16, 128, T], BF16)
    scr("u_scr", [T, D], F32)
    scr("bout_scr", [2, D], BF16)
    scr("sc_down_hi", [8, 128, 2048], BF16)
    scr("sc_down_lo", [8, 128, 2048], BF16)
    scr("sc_w1_hi", [168, 128, 1024], BF16)
    scr("sc_w1_lo", [168, 128, 1024], BF16)
    scr("sc_w2_hi", [84, 128, 2048], BF16)
    scr("sc_w2_lo", [84, 128, 2048], BF16)
    scr("sc_up_hi", [16, 128, 1024], BF16)
    scr("sc_up_lo", [16, 128, 1024], BF16)
    scr("sc_core_hi", [16, 128, 2048], BF16)
    scr("sc_core_lo", [16, 128, 2048], BF16)

    p = {}
    for name, shape, dt_ in [
        ("identbf", [128, 128], BF16), ("ident32", [128, 128], F32),
        ("rank8", [128, 8], F32), ("ones_bf", [1, 128], BF16),
        ("sel8", [9, 8 * 128], BF16),
        ("bqkv_cols", [128, 48], F32), ("db_cols", [128, 8], F32),
        ("sb1_cols", [128, 8], F32), ("b1_cols", [128, 160], F32),
        ("ubcb_cols", [128, 16], F32),
        ("rb_eff_hi", [1, 8], BF16), ("rb_eff_lo", [1, 8], BF16),
        ("rw_hi", [128, 128], BF16), ("rw_lo", [128, 128], BF16),
        ("bmat_hi", [9, LHID], BF16), ("bmat_lo", [9, LHID], BF16),
    ]:
        p[name] = k.sb(name, shape, dt_)

    b1_off = []
    off = 0
    for he in HIDS:
        b1_off.append(off)
        off += he // 128

    phase0(k, g, p, b1_off)
    k.barrier()
    pst = make_prep(k, g, p, b1_off)
    if BUILD_PHASES >= 1:
        phase1(k, g, p)
    run_prep(k, 10 ** 9)
    k.barrier()
    pst.close()
    if BUILD_PHASES >= 2:
        phase2(k, g, p)
        k.barrier()
    if BUILD_PHASES >= 3:
        phase3(k, g, p)
        k.barrier()
    if BUILD_PHASES >= 4:
        phase4(k, g, p, b1_off)

    for s in k.dma_sems:
        if s.n:
            k.wait("gpsimd", s, s.n)
            k.wait("tensor", s, s.n)

    with nc.Block() as block:
        @block.tensor
        def _(e):
            for fn in k.q["tensor"]:
                fn(e)

        @block.vector
        def _(e):
            for fn in k.q["vector"]:
                fn(e)

        @block.scalar
        def _(e):
            for fn in k.q["scalar"]:
                fn(e)

        @block.sync
        def _(e):
            for fn in k.q["sync"]:
                fn(e)

        @block.gpsimd
        def _(e):
            for fn in k.q["gpsimd"]:
                fn(e)

    k.es.close()
    return nc


def phase0(k, g, p, b1_off):
    st = contextlib.ExitStack()
    for name, shape, dt_ in [
        ("bout_row", [1, D], F32), ("rb_eff", [1, 8], F32),
        ("brow_s", [1, 6144], F32), ("smallcol", [128, 64], F32),
    ]:
        p[name] = k.sb(name, shape, dt_, stack=st)
    sP0 = k.dma_sem("sP0")
    k.load(p["identbf"][:, :], g["c_identbf"][:, :], sP0)
    k.load(p["ident32"][:, :], g["c_ident32"][:, :], sP0)
    k.load(p["rank8"][:, :], g["c_rank8"][:, :], sP0)
    k.load(p["ones_bf"][:, :], g["c_ones_bf"][:, :], sP0)
    k.load(p["sel8"][:, :], g["c_sel"][:, :], sP0)

    p0a = k.ps("p0a", [128, 512], F32, stack=st)
    brow = p["brow_s"]

    def rows_to_cols(src_row_hbm, n, dst_cols, dst_off=0):
        lv = k.load(brow[:, 0:n * 128], src_row_hbm, sP0)
        k.wait("tensor", sP0, lv)
        tv = None
        for j in range(n):
            tv = k.op("tensor", lambda e, pp=p0a[:, j:j + 1],
                      w=brow[:, j * 128:(j + 1) * 128],
                      m=p["ident32"][0:1, 0:1], s=(j == 0), t=(j == n - 1):
                      e.matmul(pp, w, m, start=s, stop=t, skip_group_check=True),
                      inc=k.sTN if j == n - 1 else None)
        k.wait("vector", k.sTN, tv)
        ev = k.vec(lambda e, d=dst_cols[:, dst_off:dst_off + n],
                  s=p0a[:, 0:n]: e.tensor_copy(d, s), inc=k.sVE)
        k.wait("tensor", k.sVE, ev)
        k.wait("sync", k.sVE, ev)
        return ev

    rows_to_cols(g["b_qkv"][:, :], 48, p["bqkv_cols"])
    rows_to_cols(g["down_b"][:, :], 8, p["db_cols"])
    rows_to_cols(g["shared_b1"][:, :], 8, p["sb1_cols"])
    for i in range(8):
        rows_to_cols(g[f"e{i}_b1"][:, :], HIDS[i] // 128, p["b1_cols"], b1_off[i])
    rows_to_cols(g["up_b"][:, :], 16, p["ubcb_cols"])
    ev = rows_to_cols(g["core_b"][:, :], 16, p["smallcol"])
    k.wait("vector", k.sVE, ev)
    k.vec(lambda e: e.tensor_tensor(
        p["ubcb_cols"][:, 0:16], p["ubcb_cols"][:, 0:16],
        p["smallcol"][:, 0:16], OP.add))

    for name, shape, dt_ in [("bout_row_hi", [1, D], BF16),
                             ("bout_row_lo", [1, D], BF16)]:
        p[name] = k.sb(name, shape, dt_, stack=st)
    lv = k.load(p["bout_row"][:, :], g["b_out"][:, :], sP0)
    k.wait("vector", sP0, lv)
    bsp = k.split(p["bout_row_hi"][:, :], p["bout_row_lo"][:, :],
                  p["bout_row"][:, :])
    k.wait("gpsimd", k.sVE, bsp)
    sBout = k.dma_sem("sBout")
    k.store(g["bout_scr"][0:1, :], p["bout_row_hi"][:, :], sBout)
    k.store(g["bout_scr"][1:2, :], p["bout_row_lo"][:, :], sBout)

    k.load(brow[:, 0:8], g["router_b"][:, :], sP0)
    k.load(brow[:, 128:136], g["c_neglc_hi"][:, :], sP0)
    lv = k.load(brow[:, 256:264], g["c_neglc_lo"][:, :], sP0)
    k.wait("vector", sP0, lv)
    k.vec(lambda e: e.tensor_tensor(
        p["rb_eff"][:, :], brow[0:1, 0:8], brow[0:1, 128:136], OP.add))
    k.vec(lambda e: e.tensor_tensor(
        p["rb_eff"][:, :], p["rb_eff"][:, :], brow[0:1, 256:264], OP.add))
    k.split(p["rb_eff_hi"][:, :], p["rb_eff_lo"][:, :], p["rb_eff"][:, :])

    rwnat = k.sb("rwnat", [128, 128], F32, stack=st)
    rw_view = g["router_w"].rearrange("(kc pp) o -> pp kc o", pp=128)
    lv = k.load(rwnat.rearrange("pp (kc o) -> pp kc o", kc=16), rw_view, sP0)
    k.wait("vector", sP0, lv)
    k.split(p["rw_hi"][:, :], p["rw_lo"][:, :], rwnat[:, :])

    st.close()


def make_prep(k, g, p, b1_off):
    pst = contextlib.ExitStack()
    w2s = k.sb("w2s", [128, 2048], F32, stack=pst)
    w2s_hi = k.sb("w2s_hi", [128, 2048], BF16, stack=pst)
    w2s_lo = k.sb("w2s_lo", [128, 2048], BF16, stack=pst)
    b2row = k.sb("b2row", [1, LHID], F32, stack=pst)
    crow = k.sb("crow", [1, LHID], F32, stack=pst)
    prow = k.sb("prow", [1, LHID], F32, stack=pst)
    bmat = k.sb("bmat", [9, LHID], F32, stack=pst)
    gb1_cols = k.sb("gb1_cols", [128, 160], F32, stack=pst)
    gb1_hi = k.sb("gb1_hi", [128, 160], BF16, stack=pst)
    gb1_lo = k.sb("gb1_lo", [128, 160], BF16, stack=pst)
    p0b = k.ps("p0b", [1, LHID], F32, stack=pst)
    sW2 = [k.dma_sem("sW2a"), k.dma_sem("sW2b")]
    sPb = k.dma_sem("sPb")
    stPr = k.dma_sem("stPr")

    k.wait("scalar", k.sVE, k.sVE.n)
    sv = k.sca(lambda e: e.activation(
        gb1_cols[:, 0:160], p["b1_cols"][:, 0:160], AF.Gelu), inc=k.sSC)
    k.wait("vector", k.sSC, sv)
    k.split(gb1_hi[:, :], gb1_lo[:, :], gb1_cols[:, :])
    k.vec(lambda e: e.memset(crow[:, :], 0.0))

    w1base, w2base = [], []
    o1 = o2 = 0
    for he in HIDS:
        w1base.append(o1)
        w2base.append(o2)
        o1 += he // 128
        o2 += he // 256
    shared_w1base, shared_w2base = o1, o2

    st_state = {"slot": 0, "split": [0, 0], "store": [0, 0], "mm": [0, 0]}
    jobs = []

    def chunk_job(src_ap, w, dh, dl, cmms=None):
        def emit():
            sl = st_state["slot"]
            st_state["slot"] ^= 1
            k.wait("scalar", k.sVE, st_state["split"][sl])
            lv = k.load(w2s[:, sl * 1024:sl * 1024 + w], src_ap, sW2[sl],
                        queue="scalar")
            k.wait("vector", sW2[sl], lv)
            k.wait("vector", stPr, st_state["store"][sl])
            if cmms is not None:
                k.wait("vector", k.sTN, st_state["mm"][sl])
            sd = k.split(w2s_hi[:, sl * 1024:sl * 1024 + w],
                         w2s_lo[:, sl * 1024:sl * 1024 + w],
                         w2s[:, sl * 1024:sl * 1024 + w])
            st_state["split"][sl] = sd
            k.wait("gpsimd", k.sVE, sd)
            srch = w2s_hi[:, sl * 1024:sl * 1024 + w]
            srcl = w2s_lo[:, sl * 1024:sl * 1024 + w]
            if len(dh.shape) == 3:
                srch = srch.rearrange("pp (s c) -> pp s c", c=128)
                srcl = srcl.rearrange("pp (s c) -> pp s c", c=128)
            k.store(dh, srch, stPr)
            st_state["store"][sl] = k.store(dl, srcl, stPr)
            if cmms is not None:
                ei, kc, kcs = cmms
                k.wait("tensor", k.sVE, sd)
                for half in range(2):
                    slc = slice(sl * 1024 + half * 512,
                                sl * 1024 + (half + 1) * 512)
                    st_state["mm"][sl] = k.mm3(
                        p0b[:, half * 512:(half + 1) * 512],
                        gb1_hi[:, b1_off[ei] + kc:b1_off[ei] + kc + 1],
                        gb1_lo[:, b1_off[ei] + kc:b1_off[ei] + kc + 1],
                        w2s_hi[:, slc], w2s_lo[:, slc],
                        start=(kc == 0), stop=(kc == kcs - 1),
                        inc=k.sTN if half == 1 else None)
        return emit

    def expert_finish_job(ei):
        def emit():
            k.wait("vector", k.sTN, max(st_state["mm"]))
            ev = k.vec(lambda e: e.tensor_copy(prow[:, :], p0b[:, :]))
            k.wait("tensor", k.sVE, ev)
            lv = k.load(b2row[:, :], g[f"e{ei}_b2"][:, :], sPb)
            k.wait("vector", sPb, lv)
            k.vec(lambda e: e.tensor_tensor(
                crow[:, :], crow[:, :], prow[:, :], OP.add))
            k.vec(lambda e: e.tensor_tensor(
                crow[:, :], crow[:, :], b2row[0:1, :], OP.add))
            nv = k.vec(lambda e: e.tensor_scalar(
                prow[:, :], prow[:, :], -1.0, None, OP.mult))
            k.wait("sync", k.sVE, nv)
            rowlv = k.load(bmat[ei:ei + 1, :], prow[:, :], sPb)
            k.wait("vector", sPb, rowlv)
        return emit

    def finale_job():
        def emit():
            lv = k.load(b2row[:, :], g["shared_b2"][:, :], sPb)
            k.wait("vector", sPb, lv)
            nv = k.vec(lambda e: e.scalar_tensor_tensor(
                prow[:, :], b2row[0:1, :], 0.1, crow[:, :], OP.mult, OP.add))
            k.wait("sync", k.sVE, nv)
            lv = k.load(bmat[8:9, :], prow[:, :], sPb)
            k.wait("vector", sPb, lv)
            k.split(p["bmat_hi"][:, :], p["bmat_lo"][:, :], bmat[:, :])
        return emit

    for ei in range(8):
        kcs = HIDS[ei] // 128
        for kc in range(kcs):
            jsl = w2base[ei] + kc // 2
            hf = (kc % 2) * LHID
            jobs.append(chunk_job(
                g[f"e{ei}_w2"][kc * 128:(kc + 1) * 128, :], 1024,
                g["sc_w2_hi"][jsl][:, hf:hf + LHID],
                g["sc_w2_lo"][jsl][:, hf:hf + LHID],
                cmms=(ei, kc, kcs)))
        jobs.append(expert_finish_job(ei))
    jobs.append(finale_job())
    for kc in range(8):
        jsl = shared_w2base + kc // 2
        hf = (kc % 2) * LHID
        jobs.append(chunk_job(
            g["shared_w2"][kc * 128:(kc + 1) * 128, :], 1024,
            g["sc_w2_hi"][jsl][:, hf:hf + LHID],
            g["sc_w2_lo"][jsl][:, hf:hf + LHID]))
    for kc in range(16):
        dh = g["sc_down_hi"].rearrange(
            "oc pp (kk c) -> pp oc kk c", c=128)[:, :, kc, :]
        dl = g["sc_down_lo"].rearrange(
            "oc pp (kk c) -> pp oc kk c", c=128)[:, :, kc, :]
        jobs.append(chunk_job(g["down_w"][kc * 128:(kc + 1) * 128, :],
                              1024, dh, dl))
    for kc in range(8):
        dh = g["sc_w1_hi"].rearrange("s pp (kk c) -> pp s kk c", c=128)[
            :, shared_w1base:shared_w1base + 8, kc, :]
        dl = g["sc_w1_lo"].rearrange("s pp (kk c) -> pp s kk c", c=128)[
            :, shared_w1base:shared_w1base + 8, kc, :]
        jobs.append(chunk_job(g["shared_w1"][kc * 128:(kc + 1) * 128, :],
                              1024, dh, dl))
    for ei in range(8):
        he = HIDS[ei]
        for kc in range(8):
            for pc in range(he // 1024):
                s0 = w1base[ei] + pc * 8
                dh = g["sc_w1_hi"].rearrange(
                    "s pp (kk c) -> pp s kk c", c=128)[:, s0:s0 + 8, kc, :]
                dl = g["sc_w1_lo"].rearrange(
                    "s pp (kk c) -> pp s kk c", c=128)[:, s0:s0 + 8, kc, :]
                jobs.append(chunk_job(
                    g[f"e{ei}_w1"][kc * 128:(kc + 1) * 128,
                                   pc * 1024:(pc + 1) * 1024], 1024, dh, dl))
    for kc in range(8):
        for pc in range(2):
            dh = g["sc_up_hi"].rearrange(
                "s pp (kk c) -> pp s kk c", c=128)[:, pc * 8:pc * 8 + 8, kc, :]
            dl = g["sc_up_lo"].rearrange(
                "s pp (kk c) -> pp s kk c", c=128)[:, pc * 8:pc * 8 + 8, kc, :]
            jobs.append(chunk_job(
                g["up_w"][kc * 128:(kc + 1) * 128, pc * 1024:(pc + 1) * 1024],
                1024, dh, dl))
    for kc in range(16):
        for pc in range(2):
            dh = g["sc_core_hi"].rearrange(
                "s pp (kk c) -> pp s kk c", c=128)[:, pc * 8:pc * 8 + 8, kc, :]
            dl = g["sc_core_lo"].rearrange(
                "s pp (kk c) -> pp s kk c", c=128)[:, pc * 8:pc * 8 + 8, kc, :]
            jobs.append(chunk_job(
                g["core_w"][kc * 128:(kc + 1) * 128,
                            pc * 1024:(pc + 1) * 1024], 1024, dh, dl))
    k.prep_jobs = jobs
    k.prep_i = 0
    return pst


def run_prep(k, n):
    while n > 0 and k.prep_i < len(k.prep_jobs):
        k.prep_jobs[k.prep_i]()
        k.prep_i += 1
        n -= 1


def phase1(k, g, p):
    st = contextlib.ExitStack()
    x_fm_hi_t = k.sb("x_fm_hi", [128, 16 * T], BF16, stack=st)
    x_fm_lo_t = k.sb("x_fm_lo", [128, 16 * T], BF16, stack=st)
    st1 = contextlib.ExitStack()
    xtm = k.sb("xtm", [128, 2 * D], F32, stack=st1)
    xtm_hi = k.sb("xtm_hi", [128, 2 * D], BF16, stack=st1)
    xtm_lo = k.sb("xtm_lo", [128, 2 * D], BF16, stack=st1)
    k.ptr = k.ps("ptr", [128, 2048], BF16, stack=st)
    k.ptr_view = k.ptr.rearrange("pp (bk j c) -> pp bk j c", bk=2, j=8)
    k.ptr_last = [0, 0]
    identbf = p["identbf"]
    x_fm_hi = x_fm_hi_t.rearrange("pp (hc t) -> pp hc t", hc=16)
    x_fm_lo = x_fm_lo_t.rearrange("pp (hc t) -> pp hc t", hc=16)

    sX = [k.dma_sem("sXa"), k.dma_sem("sXb")]
    split_done = {}
    trans_done = {}
    for tc in range(16):
        slot = tc % 2
        if tc >= 2:
            k.wait("sync", k.sVE, split_done[tc - 2])
        lv = k.load(xtm[:, slot * D:(slot + 1) * D],
                    g["x"][tc * 128:(tc + 1) * 128, :], sX[slot])
        k.wait("vector", sX[slot], lv)
        if tc >= 2:
            k.wait("vector", k.sTN, trans_done[tc - 2])
        sd = k.split(xtm_hi[:, slot * D:(slot + 1) * D],
                     xtm_lo[:, slot * D:(slot + 1) * D],
                     xtm[:, slot * D:(slot + 1) * D])
        split_done[tc] = sd
        k.wait("tensor", k.sVE, sd)
        for gi in range(4):
            src = xtm_hi if gi < 2 else xtm_lo
            hc0 = (gi % 2) * 8
            srcs = [src[:, slot * D + (hc0 + j) * 128:slot * D + (hc0 + j + 1) * 128]
                    for j in range(8)]
            dst = (x_fm_hi if gi < 2 else x_fm_lo)[:, hc0:hc0 + 8,
                                                   tc * 128:(tc + 1) * 128]
            tv, ev = k.tgroup(srcs, dst, identbf)
            if gi == 3:
                trans_done[tc] = tv
    k.barrier()
    st1.close()

    wnat = k.sb("wnat", [128, D], F32, stack=st)
    wnat_hi = k.sb("wnat_hi", [128, D], BF16, stack=st)
    wnat_lo = k.sb("wnat_lo", [128, D], BF16, stack=st)
    wt_hi = k.sb("wt_hi", [128, D], BF16, stack=st)
    wt_lo = k.sb("wt_lo", [128, D], BF16, stack=st)
    qkstage_hi = k.sb("qkstage_hi", [128, 1024], BF16, stack=st)
    qkstage_lo = k.sb("qkstage_lo", [128, 1024], BF16, stack=st)
    vstage = k.sb("vstage", [128, 512], F32, stack=st)
    vstage_hi = k.sb("vstage_hi", [128, 512], BF16, stack=st)
    vstage_lo = k.sb("vstage_lo", [128, 512], BF16, stack=st)
    vt_hi = k.sb("vt_hi", [128, 512], BF16, stack=st)
    vt_lo = k.sb("vt_lo", [128, 512], BF16, stack=st)
    pq = k.ps("pq", [128, 2048], F32, stack=st)

    sWq = [k.dma_sem("sWqa"), k.dma_sem("sWqb")]
    stQK = [k.dma_sem("stQKa"), k.dma_sem("stQKb")]
    stV = k.dma_sem("stV")
    wsplit, wtrans, mmdone, pair_evac = {}, {}, {}, {}
    qk_store_last = [0, 0]
    v_store_last = 0
    wt_hi_v = wt_hi.rearrange("pp (hc c) -> pp hc c", hc=16)
    wt_lo_v = wt_lo.rearrange("pp (hc c) -> pp hc c", hc=16)
    for ob in range(48):
        if ob >= 1:
            k.wait("sync", k.sVE, wsplit[ob - 1])
        lv = k.load(wnat[:, :], g["w_qkv"][ob * 128:(ob + 1) * 128, :],
                    sWq[ob % 2])
        k.wait("vector", sWq[ob % 2], lv)
        if ob >= 1:
            k.wait("vector", k.sTN, wtrans[ob - 1])
        sd = k.split(wnat_hi[:, :], wnat_lo[:, :], wnat[:, :])
        wsplit[ob] = sd
        k.wait("tensor", k.sVE, sd)
        wtev = 0
        for gi in range(4):
            src = wnat_hi if gi < 2 else wnat_lo
            dstv = wt_hi_v if gi < 2 else wt_lo_v
            hc0 = (gi % 2) * 8
            srcs = [src[:, (hc0 + j) * 128:(hc0 + j + 1) * 128]
                    for j in range(8)]
            tv, wtev = k.tgroup(srcs, dstv[:, hc0:hc0 + 8, :], identbf)
            if gi == 3:
                wtrans[ob] = tv
        k.wait("tensor", k.sVE, wtev)
        for pair in range(2):
            if ob >= 1:
                k.wait("tensor", k.sVE, pair_evac.get((ob - 1, pair), 0))
            tv = None
            for hc in range(16):
                whid = wt_hi_v[:, hc, :]
                wlod = wt_lo_v[:, hc, :]
                for sub in (2 * pair, 2 * pair + 1):
                    psl = pq[:, sub * 512:(sub + 1) * 512]
                    xh = x_fm_hi[:, hc, sub * 512:(sub + 1) * 512]
                    xl = x_fm_lo[:, hc, sub * 512:(sub + 1) * 512]
                    last = (hc == 15 and sub == 2 * pair + 1)
                    tv = k.mm3(psl, whid, wlod, xh, xl,
                               start=(hc == 0), stop=(hc == 15),
                               inc=k.sTN if last else None)
            if pair == 1:
                mmdone[ob] = tv
            k.wait("vector", k.sTN, tv)
            for sub in (2 * pair, 2 * pair + 1):
                psl = pq[:, sub * 512:(sub + 1) * 512]
                bias = p["bqkv_cols"][:, ob:ob + 1]
                ss = sub % 2
                if ob < 32:
                    qh = qkstage_hi[:, ss * 512:(ss + 1) * 512]
                    ql = qkstage_lo[:, ss * 512:(ss + 1) * 512]
                    k.wait("vector", stQK[ss], qk_store_last[ss])
                    k.vec(lambda e, pp=psl, bi=bias, d=qh:
                         e.tensor_scalar(d, pp, bi, None, OP.add))
                    ev = k.vec(lambda e, pp=psl, bi=bias, d=ql, h=qh:
                              e.scalar_tensor_tensor(d, pp, bi, h,
                                                     OP.add, OP.subtract),
                              inc=k.sVE)
                    pair_evac[(ob, pair)] = ev
                    dc = ob % 16
                    scr_h = g["q_scr_hi"] if ob < 16 else g["k_scr_hi"]
                    scr_l = g["q_scr_lo"] if ob < 16 else g["k_scr_lo"]
                    k.wait("gpsimd", k.sVE, ev)
                    k.store(scr_h[dc][:, sub * 512:(sub + 1) * 512], qh, stQK[ss])
                    qk_store_last[ss] = k.store(
                        scr_l[dc][:, sub * 512:(sub + 1) * 512], ql, stQK[ss])
                else:
                    dc = ob - 32
                    k.wait("vector", stV, v_store_last)
                    ev = k.vec(lambda e, pp=psl, bi=bias:
                              e.tensor_scalar(vstage[:, :], pp, bi, None, OP.add),
                              inc=k.sVE)
                    pair_evac[(ob, pair)] = ev
                    sd2 = k.split(vstage_hi[:, :], vstage_lo[:, :], vstage[:, :])
                    k.wait("tensor", k.sVE, sd2)
                    vth = vt_hi.rearrange("pp (j c) -> pp j c", j=4)
                    vtl = vt_lo.rearrange("pp (j c) -> pp j c", j=4)
                    _, evh = k.tgroup([vstage_hi[:, j * 128:(j + 1) * 128]
                                       for j in range(4)], vth, identbf)
                    _, evl = k.tgroup([vstage_lo[:, j * 128:(j + 1) * 128]
                                       for j in range(4)], vtl, identbf)
                    k.wait("gpsimd", k.sVE, evl)
                    vh = g["v_scr_hi"][dc].rearrange("(a bb) d -> bb a d", bb=128)[
                        :, sub * 4:(sub + 1) * 4, :]
                    vl = g["v_scr_lo"][dc].rearrange("(a bb) d -> bb a d", bb=128)[
                        :, sub * 4:(sub + 1) * 4, :]
                    k.store(vh, vth, stV)
                    v_store_last = k.store(vl, vtl, stV)
        run_prep(k, 9)
    st.close()


def phase2(k, g, p):
    st = contextlib.ExitStack()
    identbf = p["identbf"]
    K_hi = k.sb("K_hi", [128, 16 * T], BF16, stack=st)
    K_lo = k.sb("K_lo", [128, 16 * T], BF16, stack=st)
    qsb_hi = k.sb("qsb_hi", [128, 2 * 16 * 128], BF16, stack=st)
    qsb_lo = k.sb("qsb_lo", [128, 2 * 16 * 128], BF16, stack=st)
    probs = k.sb("probs", [128, T], F32, stack=st)
    probs_hi = k.sb("probs_hi", [128, T], BF16, stack=st)
    probs_lo = k.sb("probs_lo", [128, T], BF16, stack=st)
    pT_hi = k.sb("pT_hi", [128, 16 * 256], BF16, stack=st)
    pT_lo = k.sb("pT_lo", [128, 16 * 256], BF16, stack=st)
    vsb_hi = k.sb("vsb_hi", [128, 2 * 2048], BF16, stack=st)
    vsb_lo = k.sb("vsb_lo", [128, 2 * 2048], BF16, stack=st)
    ctxs_hi = k.sb("ctxs_hi", [128, 256], BF16, stack=st)
    ctxs_lo = k.sb("ctxs_lo", [128, 256], BF16, stack=st)
    cols = k.sb("p2cols", [128, 8], F32, stack=st)
    psA = k.ps("psA", [128, 2048], F32, stack=st)
    psB = k.ps("psB", [128, 2048], F32, stack=st)
    KV = K_hi.rearrange("pp (dc t) -> pp dc t", dc=16)
    KL = K_lo.rearrange("pp (dc t) -> pp dc t", dc=16)
    QV = qsb_hi.rearrange("pp (sl dc c) -> pp sl dc c", sl=2, dc=16)
    QL = qsb_lo.rearrange("pp (sl dc c) -> pp sl dc c", sl=2, dc=16)
    PTV = pT_hi.rearrange("pp (kc c) -> pp kc c", kc=16)
    PTL = pT_lo.rearrange("pp (kc c) -> pp kc c", kc=16)
    VV = vsb_hi.rearrange("pp (sl kc c) -> pp sl kc c", sl=2, kc=16)
    VL = vsb_lo.rearrange("pp (sl kc c) -> pp sl kc c", sl=2, kc=16)
    k.ptr = psA[:, 0:1024].bitcast(BF16)
    k.ptr_view = k.ptr.rearrange("pp (bk j c) -> pp bk j c", bk=2, j=8)
    k.ptr_last = [0, 0]
    k.tg_bank = 0

    sK = [k.dma_sem("sKa"), k.dma_sem("sKb")]
    for dc in range(16):
        k.load(KV[:, dc, :], g["k_scr_hi"][dc], sK[dc % 2])
        k.load(KL[:, dc, :], g["k_scr_lo"][dc], sK[dc % 2])
    kload = [(s, s.n) for s in sK]
    sQ = [k.dma_sem("sQa"), k.dma_sem("sQb")]
    sV = [k.dma_sem("sVa"), k.dma_sem("sVb")]
    stC = k.dma_sem("stC")

    qload = {}
    scores_done = {}
    exp_done = {}
    ctx_mm_done = [0, 0]
    ctx_evac = [0, 0]
    ctx_store = [0, 0]
    pt_evac_last = [0]

    def load_q(s, qc):
        slot = (2 * s + qc) % 2
        q0 = s * 256 + qc * 128
        prev = scores_done.get(2 * s + qc - 2)
        if prev:
            k.wait("sync", k.sTN, prev)
        qload[2 * s + qc] = (
            k.load(QV[:, slot], g["q_scr_hi"].rearrange(
                "dc pp t -> pp dc t")[:, :, q0:q0 + 128], sQ[slot]),
            k.load(QL[:, slot], g["q_scr_lo"].rearrange(
                "dc pp t -> pp dc t")[:, :, q0:q0 + 128], sQ[slot]))

    load_q(0, 0)
    load_q(0, 1)
    for s in range(8):
        for qc in range(2):
            idx = 2 * s + qc
            slot = idx % 2
            P = psA if qc == 0 else psB
            lvh, lvl = qload[idx]
            k.wait("tensor", sQ[slot], lvl)
            if s == 0:
                for sem, v in kload:
                    k.wait("tensor", sem, v)
            if qc == 0:
                k.wait("tensor", k.sVE, max(k.ptr_last))
                if idx >= 2:
                    k.wait("tensor", k.sSC, exp_done[idx - 2])
            else:
                k.wait("tensor", k.sVE, max(ctx_evac))
                if idx >= 2:
                    k.wait("tensor", k.sSC, exp_done[idx - 2])
            tv = None
            for dc in range(16):
                for kb in range(4):
                    tv = k.mm3(P[:, kb * 512:(kb + 1) * 512],
                               QV[:, slot, dc, :], QL[:, slot, dc, :],
                               KV[:, dc, kb * 512:(kb + 1) * 512],
                               KL[:, dc, kb * 512:(kb + 1) * 512],
                               start=(dc == 0), stop=(dc == 15),
                               inc=k.sTN if (dc == 15 and kb == 3) else None)
            scores_done[idx] = tv
            if idx + 2 < 16:
                load_q((idx + 2) // 2, (idx + 2) % 2)
        for qc in range(2):
            idx = 2 * s + qc
            P = psA if qc == 0 else psB
            tv = scores_done[idx]
            k.wait("vector", k.sTN, tv)
            k.vec(lambda e, P=P: e.reduce_max(cols[:, 0:1], P, axis=AX.X))
            nm = k.vec(lambda e: e.tensor_scalar(
                cols[:, 1:2], cols[:, 0:1], -INV_S_HI, None, OP.mult))
            k.wait("scalar", k.sVE, nm)
            k.wait("scalar", k.sTN, tv)
            ev = k.sca(lambda e, P=P: e.activation(
                probs[:, :], P, AF.Exp, bias=cols[:, 1:2], scale=INV_S_HI,
                accum_out=cols[:, 2:3]))
            exp_done[idx] = ev
            k.wait("vector", k.sSC, ev)
            k.vec(lambda e: e.reciprocal(cols[:, 3:4], cols[:, 2:3]))
            k.vec(lambda e: e.tensor_scalar(
                probs_hi[:, :], probs[:, :], cols[:, 3:4], None, OP.mult))
            sd = k.vec(lambda e: e.scalar_tensor_tensor(
                probs_lo[:, :], probs[:, :], cols[:, 3:4], probs_hi[:, :],
                OP.mult, OP.subtract))
            k.wait("tensor", k.sVE, sd)
            k.wait("tensor", k.sSC, ev)
            if qc == 0:
                k.wait("vector", k.sTN, max(ctx_mm_done))
            for gi in range(4):
                srcp = probs_hi if gi < 2 else probs_lo
                dstv = PTV if gi < 2 else PTL
                kc0 = (gi % 2) * 8
                srcs = [srcp[:, (kc0 + j) * 128:(kc0 + j + 1) * 128]
                        for j in range(8)]
                dst = dstv[:, kc0:kc0 + 8, qc * 128:(qc + 1) * 128]
                tv2, ev2 = k.tgroup(srcs, dst, identbf)
            pt_evac_last[0] = ev2
        vload = [0, 0]
        if ctx_mm_done[0]:
            k.wait("sync", k.sTN, ctx_mm_done[0])
        k.load(VV[:, 0], g["v_scr_hi"][0].rearrange(
            "(kc pp) d -> pp kc d", pp=128), sV[0])
        vload[0] = k.load(VL[:, 0], g["v_scr_lo"][0].rearrange(
            "(kc pp) d -> pp kc d", pp=128), sV[0])
        for dc in range(16):
            slot = dc % 2
            if dc + 1 < 16:
                nsl = (dc + 1) % 2
                if ctx_mm_done[nsl]:
                    k.wait("sync", k.sTN, ctx_mm_done[nsl])
                k.load(VV[:, nsl], g["v_scr_hi"][dc + 1].rearrange(
                    "(kc pp) d -> pp kc d", pp=128), sV[nsl])
                vload[nsl] = k.load(VL[:, nsl], g["v_scr_lo"][dc + 1].rearrange(
                    "(kc pp) d -> pp kc d", pp=128), sV[nsl])
            ctxp = psB[:, 512 * slot + 0:512 * slot + 256]
            k.wait("tensor", sV[slot], vload[slot])
            k.wait("tensor", k.sVE, pt_evac_last[0])
            k.wait("tensor", k.sVE, ctx_evac[slot])
            tv = None
            for kc in range(16):
                tv = k.mm3(ctxp, VV[:, slot, kc, :], VL[:, slot, kc, :],
                           PTV[:, kc, :], PTL[:, kc, :],
                           start=(kc == 0), stop=(kc == 15),
                           inc=k.sTN if kc == 15 else None)
            ctx_mm_done[slot] = tv
            k.wait("vector", k.sTN, tv)
            k.wait("vector", stC, ctx_store[0])
            cse = k.split(ctxs_hi[:, :], ctxs_lo[:, :], ctxp)
            ctx_evac[slot] = cse
            k.wait("gpsimd", k.sVE, cse)
            q0 = s * 256
            k.store(g["ctx_scr_hi"][dc][:, q0:q0 + 256], ctxs_hi[:, :], stC)
            ctx_store[0] = k.store(
                g["ctx_scr_lo"][dc][:, q0:q0 + 256], ctxs_lo[:, :], stC)
    st.close()


def phase3(k, g, p):
    st = contextlib.ExitStack()
    identbf = p["identbf"]
    p["bout_row_hi"] = k.sb("p3bout_hi", [1, D], BF16, stack=st)
    p["bout_row_lo"] = k.sb("p3bout_lo", [1, D], BF16, stack=st)
    sB3 = k.dma_sem("sB3")
    k.load(p["bout_row_hi"][:, :], g["bout_scr"][0:1, :], sB3)
    k.load(p["bout_row_lo"][:, :], g["bout_scr"][1:2, :], sB3)
    wot_hi = k.sb("wot_hi", [128, 16 * 2048], BF16, stack=st)
    wot_lo = k.sb("wot_lo", [128, 16 * 2048], BF16, stack=st)
    WOH = wot_hi.rearrange("pp (dc o) -> pp dc o", dc=16)
    WOL = wot_lo.rearrange("pp (dc o) -> pp dc o", dc=16)

    st1 = contextlib.ExitStack()
    wnat = k.sb("p3wnat", [128, 2 * D], F32, stack=st1)
    wnat_hi = k.sb("p3wnat_hi", [128, 2 * D], BF16, stack=st1)
    wnat_lo = k.sb("p3wnat_lo", [128, 2 * D], BF16, stack=st1)
    k.ptr = k.ps("p3ptr", [128, 2048], BF16, stack=st1)
    k.ptr_view = k.ptr.rearrange("pp (bk j c) -> pp bk j c", bk=2, j=8)
    k.ptr_last = [0, 0]
    k.tg_bank = 0
    sWo = [k.dma_sem("sWoa"), k.dma_sem("sWob")]
    wsplit = {}
    wtrans = {}
    for oc in range(16):
        slot = oc % 2
        if oc >= 2:
            k.wait("sync", k.sVE, wsplit[oc - 2])
        lv = k.load(wnat[:, slot * D:(slot + 1) * D],
                    g["w_out"][oc * 128:(oc + 1) * 128, :], sWo[slot])
        k.wait("vector", sWo[slot], lv)
        if oc >= 2:
            k.wait("vector", k.sTN, wtrans[oc - 2])
        sd = k.split(wnat_hi[:, slot * D:(slot + 1) * D],
                     wnat_lo[:, slot * D:(slot + 1) * D],
                     wnat[:, slot * D:(slot + 1) * D])
        wsplit[oc] = sd
        k.wait("tensor", k.sVE, sd)
        for gi in range(4):
            srct = wnat_hi if gi < 2 else wnat_lo
            dstv = WOH if gi < 2 else WOL
            dc0 = (gi % 2) * 8
            srcs = [srct[:, slot * D + (dc0 + j) * 128:
                         slot * D + (dc0 + j + 1) * 128] for j in range(8)]
            tv, ev = k.tgroup(srcs, dstv[:, dc0:dc0 + 8, oc * 128:(oc + 1) * 128],
                              identbf)
            if gi == 3:
                wtrans[oc] = tv
    k.barrier()
    st1.close()

    cxs_hi = k.sb("cxs_hi", [128, 2 * 16 * 256], BF16, stack=st)
    cxs_lo = k.sb("cxs_lo", [128, 2 * 16 * 256], BF16, stack=st)
    CXH = cxs_hi.rearrange("pp (sl dc c) -> pp sl dc c", sl=2, dc=16)
    CXL = cxs_lo.rearrange("pp (sl dc c) -> pp sl dc c", sl=2, dc=16)
    xzb = k.sb("xzb", [128, 2 * D], F32, stack=st)
    ubuf = k.sb("ubuf", [128, D], F32, stack=st)
    lncol = k.sb("lncol", [128, 8], F32, stack=st)
    poutA = k.ps("poutA", [128, 2048], F32, stack=st)
    poutB = k.ps("poutB", [128, 2048], F32, stack=st)
    sCx = [k.dma_sem("sCxa"), k.dma_sem("sCxb")]
    sXc = [k.dma_sem("sXca"), k.dma_sem("sXcb")]
    stU = k.dma_sem("stU")
    cxload = {}
    xload = {}
    mm_done = {}
    z_done = {}
    u_store = {}

    def load_cx(ts):
        slot = ts % 2
        prev = mm_done.get((ts - 2, 1))
        if prev:
            k.wait("sync", k.sTN, prev)
        t0 = ts * 256
        k.load(CXH[:, slot], g["ctx_scr_hi"].rearrange(
            "dc pp t -> pp dc t")[:, :, t0:t0 + 256], sCx[slot])
        cxload[ts] = k.load(CXL[:, slot], g["ctx_scr_lo"].rearrange(
            "dc pp t -> pp dc t")[:, :, t0:t0 + 256], sCx[slot])

    def load_x(ts, tc):
        i = 2 * ts + tc
        slot = i % 2
        prev = u_store.get(i - 2)
        if prev:
            k.wait("sync", stU, prev)
        t0 = ts * 256 + tc * 128
        xload[i] = k.load(xzb[:, slot * D:(slot + 1) * D],
                          g["x"][t0:t0 + 128, :], sXc[slot])

    load_cx(0)
    load_x(0, 0)
    load_x(0, 1)
    for ts in range(8):
        slot = ts % 2
        if ts + 1 < 8:
            load_cx(ts + 1)
        k.wait("tensor", sCx[slot], cxload[ts])
        for tc in range(2):
            i = 2 * ts + tc
            P = poutA if i % 2 == 0 else poutB
            if i >= 2:
                k.wait("tensor", k.sVE, z_done[i - 2])
            tv = None
            for ob in range(4):
                k.op("tensor", lambda e, pp=P[:, ob * 512:(ob + 1) * 512],
                     m=p["bout_row_hi"][:, ob * 512:(ob + 1) * 512]:
                     e.matmul(pp, p["ones_bf"], m, start=True, stop=False,
                              skip_group_check=True))
                k.op("tensor", lambda e, pp=P[:, ob * 512:(ob + 1) * 512],
                     m=p["bout_row_lo"][:, ob * 512:(ob + 1) * 512]:
                     e.matmul(pp, p["ones_bf"], m, start=False, stop=False,
                              skip_group_check=True))
            for dc in range(16):
                for ob in range(4):
                    tv = k.mm3(P[:, ob * 512:(ob + 1) * 512],
                               CXH[:, slot, dc, tc * 128:(tc + 1) * 128],
                               CXL[:, slot, dc, tc * 128:(tc + 1) * 128],
                               WOH[:, dc, ob * 512:(ob + 1) * 512],
                               WOL[:, dc, ob * 512:(ob + 1) * 512],
                               start=False, stop=(dc == 15),
                               inc=k.sTN if (dc == 15 and ob == 3) else None)
            mm_done[(ts, tc)] = tv
            xs = xzb[:, (i % 2) * D:(i % 2 + 1) * D]
            us = ubuf[:, 0:D]
            k.wait("vector", k.sTN, tv)
            k.wait("vector", sXc[i % 2], xload[i])
            zv = k.vec(lambda e, pp=P, x=xs, z=us: e.tensor_tensor(z, pp, x, OP.add))
            z_done[i] = zv
            k.vec(lambda e, z=us: e.reduce_sum(lncol[:, 0:1], z, axis=AX.X))
            k.wait("scalar", k.sVE, zv)
            sq = k.sca(lambda e, z=us, t=xs: e.activation(
                t, z, AF.Square, accum_out=lncol[:, 1:2]))
            k.vec(lambda e: e.tensor_scalar(
                lncol[:, 2:3], lncol[:, 0:1], 1.0 / D, None, OP.mult))
            k.wait("vector", k.sSC, sq)
            k.vec(lambda e: e.tensor_scalar(
                lncol[:, 3:4], lncol[:, 1:2], 1.0 / D, None, OP.mult))
            k.vec(lambda e: e.tensor_tensor(
                lncol[:, 4:5], lncol[:, 2:3], lncol[:, 2:3], OP.mult))
            vv = k.vec(lambda e: e.scalar_tensor_tensor(
                lncol[:, 5:6], lncol[:, 4:5], -1.0, lncol[:, 3:4],
                OP.mult, OP.add))
            vv = k.vec(lambda e: e.tensor_scalar(
                lncol[:, 5:6], lncol[:, 5:6], EPS, None, OP.add))
            k.wait("scalar", k.sVE, vv)
            sq2 = k.sca(lambda e: e.activation(
                lncol[:, 6:7], lncol[:, 5:6], AF.Sqrt))
            k.wait("vector", k.sSC, sq2)
            k.vec(lambda e: e.reciprocal(lncol[:, 7:8], lncol[:, 6:7]))
            uv = k.vec(lambda e, z=us, u=xs: e.tensor_scalar(
                u, z, lncol[:, 2:3], lncol[:, 7:8], OP.subtract, OP.mult))
            k.wait("gpsimd", k.sVE, uv)
            t0 = ts * 256 + tc * 128
            u_store[i] = k.store(g["u_scr"][t0:t0 + 128, :], xs, stU)
            if i + 2 < 16:
                load_x((i + 2) // 2, (i + 2) % 2)
    st.close()


def phase4(k, g, p, b1_off):
    st = contextlib.ExitStack()
    identbf = p["identbf"]
    ident32 = p["ident32"]
    u_tm = k.sb("u_tm", [128, 4 * D], F32, stack=st)
    g_hi = k.sb("g_hi", [128, 8 * 512], BF16, stack=st)
    g_lo = k.sb("g_lo", [128, 8 * 512], BF16, stack=st)
    acc = k.sb("acc", [128, 8 * 512], F32, stack=st)
    wsf = k.sb("wsf", [128, D], F32, stack=st)
    wsh = k.sb("wsh", [128, 4 * D], BF16, stack=st)
    wsl = k.sb("wsl", [128, 4 * D], BF16, stack=st)
    bcw = k.sb("bcw", [128, 512], F32, stack=st)
    gtmp = k.sb("gtmp", [128, 512], F32, stack=st)
    wext = k.sb("wext", [9, 512], F32, stack=st)
    wext_hi = k.sb("wext_hi", [9, 512], BF16, stack=st)
    wext_lo = k.sb("wext_lo", [9, 512], BF16, stack=st)
    lgt = k.sb("lgt", [128, 8], F32, stack=st)
    prb = k.sb("prb", [128, 8], F32, stack=st)
    prb2 = k.sb("prb2", [128, 8], F32, stack=st)
    mk = k.sb("mk", [128, 8], F32, stack=st)
    mk1 = k.sb("mk1", [128, 8], F32, stack=st)
    mk2 = k.sb("mk2", [128, 8], F32, stack=st)
    sc8 = k.sb("sc8", [128, 8], F32, stack=st)
    tmp8 = k.sb("tmp8", [128, 8], F32, stack=st)
    wt8 = k.sb("wt8", [128, 8], F32, stack=st)
    wt8b = k.sb("wt8b", [128, 8], F32, stack=st)
    cc = k.sb("cc", [128, 8], F32, stack=st)
    z2b = wsf[:, 0:D]
    PS1 = k.ps("PS1", [128, 2048], F32, stack=st)
    PS2 = k.ps("PS2", [128, 2048], F32, stack=st)
    GV_H = g_hi.rearrange("pp (kc c) -> pp kc c", kc=8)
    GV_L = g_lo.rearrange("pp (kc c) -> pp kc c", kc=8)
    ACCV = acc.rearrange("pp (kc c) -> pp kc c", kc=8)
    UV = u_tm.rearrange("pp (tc c) -> pp tc c", tc=4)

    k.ptr = PS2[:, 0:1024].bitcast(BF16)
    k.ptr_view = k.ptr.rearrange("pp (bk j c) -> pp bk j c", bk=2, j=8)
    ptr32 = PS2[:, 0:1024]
    ptr32_view = ptr32.rearrange("pp (bk j c) -> pp bk j c", bk=2, j=4)

    def tgroup32(srcs, dst):
        bank = k.tg_bank
        k.tg_bank ^= 1
        n = len(srcs)
        k.wait("tensor", k.sVE, k.ptr_last[bank])
        tv = None
        for j, s in enumerate(srcs):
            tv = k.op("tensor",
                      lambda e, pp=ptr32[:, bank * 512 + j * 128:
                                         bank * 512 + (j + 1) * 128],
                      s2=s: e.transpose(pp, s2, ident32),
                      inc=k.sTN if j == n - 1 else None)
        k.wait("vector", k.sTN, tv)
        ev = k.vec(lambda e, d=dst, s=ptr32_view[:, bank, 0:n, :]:
                   e.tensor_copy(d, s))
        k.ptr_last[bank] = ev
        return tv, ev

    sU = [k.dma_sem("sUa"), k.dma_sem("sUb")]
    sWs = [k.dma_sem(f"sWs{i}") for i in range(4)]
    sOne = k.dma_sem("sOne")
    k.load(wext[8:9, :], g["c_ones512"][:, :], sOne)
    stOut = k.dma_sem("stOut")
    k.out_sem = stOut

    class WS:
        def __init__(self):
            self.slot = 0
            self.cons = [0, 0, 0, 0]

        def feed(self, hi_src, lo_src, width):
            sl = self.slot
            self.slot = (self.slot + 1) % 4
            if self.cons[sl]:
                k.wait("sync", k.sTN, self.cons[sl])
            k.load(wsh[:, sl * D:sl * D + width], hi_src, sWs[sl])
            lv = k.load(wsl[:, sl * D:sl * D + width], lo_src, sWs[sl])
            hi = wsh[:, sl * D:sl * D + width].rearrange(
                "pp (kc c) -> pp kc c", c=128)
            lo = wsl[:, sl * D:sl * D + width].rearrange(
                "pp (kc c) -> pp kc c", c=128)
            return sl, hi, lo, lv

        def done(self, sl, tv):
            self.cons[sl] = tv

    ws = WS()
    w1base4 = []
    o1 = 0
    for he in HIDS:
        w1base4.append(o1)
        o1 += he // 128
    w2base4 = []
    o2 = 0
    for he in HIDS:
        w2base4.append(o2)
        o2 += he // 256
    bank5_evac = [0]
    t2_evac = [0, 0]
    ps1_evac = [0, 0, 0, 0]
    uload = {}
    out_store = {}
    last_final_tv = [0]

    ovl = k.sb("ovl", [128, 40960], BF16, stack=st)
    for ts in range(4):
        k.barrier()
        t0s = ts * 512
        u_fm_hi = ovl[:, 0:8192]
        u_fm_lo = ovl[:, 8192:16384]
        UFH = u_fm_hi.rearrange("pp (kc c) -> pp kc c", kc=16)
        UFL = u_fm_lo.rearrange("pp (kc c) -> pp kc c", kc=16)
        usplit = {}
        for tc in range(4):
            lv = k.load(UV[:, tc, :], g["u_scr"][t0s + tc * 128:
                                                 t0s + (tc + 1) * 128, :],
                        sU[tc % 2])
            uload[tc] = lv
        for tc in range(4):
            k.wait("vector", sU[tc % 2], uload[tc])
            sd = k.split(wsh[:, 0:D], wsl[:, 0:D], UV[:, tc, :])
            k.wait("tensor", k.sVE, sd)
            for gi in range(4):
                srct = wsh if gi < 2 else wsl
                dstv = UFH if gi < 2 else UFL
                kc0 = (gi % 2) * 8
                srcs = [srct[:, (kc0 + j) * 128:(kc0 + j + 1) * 128]
                        for j in range(8)]
                tv, ev = k.tgroup(srcs, dstv[:, kc0:kc0 + 8,
                                             tc * 128:(tc + 1) * 128], identbf)
            usplit[tc] = tv
        for _i in range(4):
            ws.cons[_i] = max(ws.cons[_i], tv)
        for tc in range(4):
            lg = PS2[:, 1024:1032]
            k.wait("tensor", k.sVE, t2_evac[0])
            k.wait("tensor", k.sVE, bank5_evac[0])
            k.op("tensor", lambda e, m=p["rb_eff_hi"]: e.matmul(
                lg, p["ones_bf"], m, start=True, stop=False,
                skip_group_check=True))
            k.op("tensor", lambda e, m=p["rb_eff_lo"]: e.matmul(
                lg, p["ones_bf"], m, start=False, stop=False,
                skip_group_check=True))
            tv = None
            for kc in range(16):
                tv = k.mm3(lg, UFH[:, kc, tc * 128:(tc + 1) * 128],
                           UFL[:, kc, tc * 128:(tc + 1) * 128],
                           p["rw_hi"][:, kc * 8:(kc + 1) * 8],
                           p["rw_lo"][:, kc * 8:(kc + 1) * 8],
                           start=False, stop=(kc == 15),
                           inc=k.sTN if kc == 15 else None)
            k.wait("vector", k.sTN, tv)
            k.vec(lambda e: e.tensor_copy(lgt[:, :], lg))
            k.vec(lambda e: e.reduce_max(cc[:, 0:1], lgt[:, :], axis=AX.X))
            nm = k.vec(lambda e: e.tensor_scalar(
                cc[:, 1:2], cc[:, 0:1], -1.0, None, OP.mult))
            k.wait("scalar", k.sVE, nm)
            sv = k.sca(lambda e: e.activation(
                prb[:, :], lgt[:, :], AF.Exp, bias=cc[:, 1:2],
                accum_out=cc[:, 2:3]))
            t2_evac[0] = max(t2_evac[0], nm)
            k.wait("vector", k.sSC, sv)
            k.vec(lambda e: e.reciprocal(cc[:, 3:4], cc[:, 2:3]))
            k.vec(lambda e: e.tensor_scalar(
                prb2[:, :], prb[:, :], cc[:, 3:4], None, OP.mult))
            k.vec(lambda e: e.reduce_max(cc[:, 0:1], prb2[:, :], axis=AX.X))
            k.vec(lambda e: e.tensor_scalar(
                mk[:, :], prb2[:, :], cc[:, 0:1], None, OP.is_equal))
            k.vec(lambda e: e.tensor_tensor(
                sc8[:, :], mk[:, :], p["rank8"][:, :], OP.mult))
            k.vec(lambda e: e.reduce_max(cc[:, 1:2], sc8[:, :], axis=AX.X))
            k.vec(lambda e: e.tensor_scalar(
                mk1[:, :], sc8[:, :], cc[:, 1:2], None, OP.is_equal))
            k.vec(lambda e: e.tensor_tensor(
                tmp8[:, :], prb2[:, :], mk1[:, :], OP.mult))
            k.vec(lambda e: e.reduce_sum(cc[:, 4:5], tmp8[:, :], axis=AX.X))
            k.vec(lambda e: e.scalar_tensor_tensor(
                sc8[:, :], mk1[:, :], -1e9, prb2[:, :], OP.mult, OP.add))
            k.vec(lambda e: e.reduce_max(cc[:, 0:1], sc8[:, :], axis=AX.X))
            k.vec(lambda e: e.tensor_scalar(
                mk[:, :], sc8[:, :], cc[:, 0:1], None, OP.is_equal))
            k.vec(lambda e: e.tensor_tensor(
                sc8[:, :], mk[:, :], p["rank8"][:, :], OP.mult))
            k.vec(lambda e: e.reduce_max(cc[:, 1:2], sc8[:, :], axis=AX.X))
            k.vec(lambda e: e.tensor_scalar(
                mk2[:, :], sc8[:, :], cc[:, 1:2], None, OP.is_equal))
            k.vec(lambda e: e.tensor_tensor(
                tmp8[:, :], prb2[:, :], mk2[:, :], OP.mult))
            k.vec(lambda e: e.reduce_sum(cc[:, 5:6], tmp8[:, :], axis=AX.X))
            dv = k.vec(lambda e: e.tensor_tensor(
                cc[:, 6:7], cc[:, 5:6], cc[:, 4:5], OP.subtract))
            k.wait("scalar", k.sVE, dv)
            sv = k.sca(lambda e: e.activation(cc[:, 7:8], cc[:, 6:7], AF.Exp))
            k.wait("vector", k.sSC, sv)
            k.vec(lambda e: e.tensor_scalar(
                cc[:, 0:1], cc[:, 7:8], 1.0, None, OP.add))
            k.vec(lambda e: e.reciprocal(cc[:, 1:2], cc[:, 0:1]))
            k.vec(lambda e: e.tensor_tensor(
                cc[:, 2:3], cc[:, 7:8], cc[:, 1:2], OP.mult))
            k.vec(lambda e: e.tensor_scalar(
                wt8[:, :], mk1[:, :], cc[:, 1:2], None, OP.mult))
            wv = k.vec(lambda e: e.scalar_tensor_tensor(
                wt8b[:, :], mk2[:, :], cc[:, 2:3], wt8[:, :], OP.mult, OP.add))
            k.wait("tensor", k.sVE, wv)
            k.wait("tensor", k.sVE, bank5_evac[0])
            ttv = k.op("tensor", lambda e: e.transpose(
                PS2[0:8, 1536:1664], wt8b[:, :], ident32), inc=k.sTN)
            k.wait("vector", k.sTN, ttv)
            bv = k.vec(lambda e, tc=tc: e.tensor_copy(
                wext[0:8, tc * 128:(tc + 1) * 128], PS2[0:8, 1536:1664]))
            t2_evac[1] = max(t2_evac[1], bv)
        k.wait("vector", sOne, 16)
        swv = k.split(wext_hi[:, :], wext_lo[:, :], wext[:, :])

        for oc in range(8):
            sl, whi, wlo, sd = ws.feed(g["sc_down_hi"][oc],
                                       g["sc_down_lo"][oc], 2048)
            k.wait("tensor", sWs[sl], sd)
            bank = oc % 2
            pt2 = PS2[:, 1024 + bank * 512:1024 + (bank + 1) * 512]
            k.wait("tensor", k.sVE, t2_evac[bank])
            tv = None
            for kc in range(16):
                tv = k.mm3(pt2, whi[:, kc, :], wlo[:, kc, :],
                           UFH[:, kc, :], UFL[:, kc, :],
                           start=(kc == 0), stop=(kc == 15),
                           inc=k.sTN if kc == 15 else None)
            ws.done(sl, tv)
            k.wait("scalar", k.sTN, tv)
            k.wait("scalar", k.sVE, t2_evac[1 - bank])
            gv = k.sca(lambda e, pp=pt2, oc=oc: e.activation(
                gtmp[:, :], pp, AF.Gelu, bias=p["db_cols"][:, oc:oc + 1]))
            k.wait("vector", k.sSC, gv)
            ev = k.split(GV_H[:, oc, :], GV_L[:, oc, :], gtmp[:, :])
            t2_evac[bank] = ev
        t3p_hi = ovl[:, 0:16384]
        t3p_lo = ovl[:, 16384:32768]
        T3H = t3p_hi.rearrange("pp (kc c) -> pp kc c", kc=32)
        T3L = t3p_lo.rearrange("pp (kc c) -> pp kc c", kc=32)

        k.wait("vector", k.sVE, swv)
        bank5_evac[0] = max(bank5_evac[0], k.ptr_last[0], k.ptr_last[1])
        for ocm in range(8):
            k.wait("tensor", k.sVE, bank5_evac[0])
            pb5 = PS2[:, 512:1024]
            tv = k.mm3(pb5, p["bmat_hi"][:, ocm * 128:(ocm + 1) * 128],
                       p["bmat_lo"][:, ocm * 128:(ocm + 1) * 128],
                       wext_hi[:, :], wext_lo[:, :],
                       start=True, stop=True, inc=k.sTN)
            k.wait("vector", k.sTN, tv)
            bank5_evac[0] = k.vec(lambda e, ocm=ocm: e.tensor_copy(
                ACCV[:, ocm, :], pb5))
            t2_evac[1] = bank5_evac[0]

        for ei in range(9):
            shared = (ei == 8)
            he = LHID if shared else HIDS[ei]
            kcs = he // 128
            w1name = "shared_w1" if shared else f"e{ei}_w1"
            w2name = "shared_w2" if shared else f"e{ei}_w2"
            bcol = p["sb1_cols"] if shared else p["b1_cols"]
            boff = 0 if shared else b1_off[ei]
            if not shared:
                k.wait("tensor", k.sVE, bank5_evac[0])
                pb5 = PS2[:, 512:1024]
                k.op("tensor", lambda e, ei=ei: e.matmul(
                    pb5, p["sel8"][:, ei * 128:(ei + 1) * 128], wext_hi[:, :],
                    start=True, stop=False, skip_group_check=True))
                tv = k.op("tensor", lambda e, ei=ei: e.matmul(
                    pb5, p["sel8"][:, ei * 128:(ei + 1) * 128], wext_lo[:, :],
                    start=False, stop=True, skip_group_check=True), inc=k.sTN)
                k.wait("vector", k.sTN, tv)
                bank5_evac[0] = k.vec(lambda e: e.tensor_copy(bcw[:, :], pb5))
            w1b = (160 if shared else w1base4[ei])
            for oc in range(kcs):
                sl, whi, wlo, sd = ws.feed(g["sc_w1_hi"][w1b + oc],
                                           g["sc_w1_lo"][w1b + oc], 1024)
                k.wait("tensor", sWs[sl], sd)
                bank = oc % 2
                pt2 = PS2[:, 1024 + bank * 512:1024 + (bank + 1) * 512]
                k.wait("tensor", k.sVE, t2_evac[bank])
                tv = None
                for kc in range(8):
                    tv = k.mm3(pt2, whi[:, kc, :], wlo[:, kc, :],
                               GV_H[:, kc, :], GV_L[:, kc, :],
                               start=(kc == 0), stop=(kc == 7),
                               inc=k.sTN if kc == 7 else None)
                ws.done(sl, tv)
                k.wait("scalar", k.sTN, tv)
                k.wait("scalar", k.sVE, t2_evac[1 - bank])
                gv = k.sca(lambda e, pp=pt2, c=bcol[:, boff + oc:boff + oc + 1]:
                           e.activation(gtmp[:, :], pp, AF.Gelu, bias=c))
                k.wait("vector", k.sSC, gv)
                if shared:
                    qv = k.vec(lambda e: e.tensor_scalar(
                        gtmp[:, :], gtmp[:, :], 0.1, None, OP.mult))
                else:
                    qv = k.vec(lambda e: e.tensor_tensor(
                        gtmp[:, :], gtmp[:, :], bcw[:, :], OP.mult))
                ev = k.split(T3H[:, oc, :], T3L[:, oc, :], gtmp[:, :])
                t2_evac[bank] = ev
            w2b = (80 if shared else w2base4[ei])
            k.wait("tensor", k.sVE, k.sVE.n)
            for grp in range(2):
                tv = None
                for kc2 in range(kcs // 2):
                    sl, whi, wlo, sd = ws.feed(g["sc_w2_hi"][w2b + kc2],
                                               g["sc_w2_lo"][w2b + kc2], 2048)
                    k.wait("tensor", sWs[sl], sd)
                    if kc2 == 0:
                        for j in range(4):
                            k.wait("tensor", k.sVE, ps1_evac[j])
                    for hf in range(2):
                        kc = 2 * kc2 + hf
                        for j in range(4):
                            oc2 = grp * 4 + j
                            pmm = PS1[:, j * 512:(j + 1) * 512]
                            tv = k.mm3(pmm, whi[:, hf * 8 + oc2, :],
                                       wlo[:, hf * 8 + oc2, :],
                                       T3H[:, kc, :], T3L[:, kc, :],
                                       start=(kc == 0),
                                       stop=(kc == kcs - 1),
                                       inc=k.sTN if (j == 3 and hf == 1)
                                       else None)
                    ws.done(sl, tv)
                k.wait("vector", k.sTN, tv)
                for j in range(4):
                    oc2 = grp * 4 + j
                    ev = k.vec(lambda e, oc2=oc2, j=j: e.tensor_tensor(
                        ACCV[:, oc2, :], ACCV[:, oc2, :],
                        PS1[:, j * 512:(j + 1) * 512], OP.add))
                    ps1_evac[j] = ev

        k.ptr_last[1] = max(k.ptr_last[1], bank5_evac[0])
        gu_hi = ovl[:, 0:8192]
        gu_lo = ovl[:, 8192:16384]
        ffn_tm = ovl[:, 16384:32768].bitcast(F32)
        acc_hi = ovl[:, 32768:36864]
        acc_lo = ovl[:, 36864:40960]
        GUH = gu_hi.rearrange("pp (kc c) -> pp kc c", kc=16)
        GUL = gu_lo.rearrange("pp (kc c) -> pp kc c", kc=16)
        AHV = acc_hi.rearrange("pp (kc c) -> pp kc c", kc=8)
        ALV = acc_lo.rearrange("pp (kc c) -> pp kc c", kc=8)
        FFV = ffn_tm.rearrange("pp (tc c) -> pp tc c", tc=4)
        k.wait("vector", k.sTN, k.sTN.n)
        sda = k.split(acc_hi[:, :], acc_lo[:, :], acc[:, :])
        k.wait("tensor", k.sVE, sda)
        gtf = wsf[:, 0:1024]
        gth = wsh[:, 0:1024]
        gtl = wsl[:, 0:1024]
        gusd = k.sVE.n
        for tc in range(4):
            for half in range(2):
                k.wait("scalar", k.sVE, gusd)
                sv = k.sca(lambda e, tc=tc, half=half: e.activation(
                    gtf, UV[:, tc, half * 1024:(half + 1) * 1024],
                    AF.Gelu))
                k.wait("vector", k.sSC, sv)
                sd = k.split(gth, gtl, gtf)
                gusd = sd
                k.wait("tensor", k.sVE, sd)
                kc0 = half * 8
                for gi in range(2):
                    srct = gth if gi == 0 else gtl
                    dstv = GUH if gi == 0 else GUL
                    srcs = [srct[:, j * 128:(j + 1) * 128] for j in range(8)]
                    tv, ev = k.tgroup(srcs, dstv[:, kc0:kc0 + 8,
                                                 tc * 128:(tc + 1) * 128],
                                      identbf)
        for _i in range(4):
            ws.cons[_i] = max(ws.cons[_i], tv)
        k.wait("tensor", k.sVE, k.sVE.n)
        fin_evac = [0, 0, 0, 0]
        for oc in range(16):
            slu, uhi, ulo, sdu = ws.feed(g["sc_up_hi"][oc],
                                         g["sc_up_lo"][oc], 1024)
            slc, chi, clo, sdc = ws.feed(g["sc_core_hi"][oc],
                                         g["sc_core_lo"][oc], 2048)
            k.wait("tensor", sWs[slu], sdu)
            k.wait("tensor", sWs[slc], sdc)
            bank = oc % 4
            pf = PS1[:, bank * 512:(bank + 1) * 512]
            k.wait("tensor", k.sVE, fin_evac[bank])
            tv = None
            for kc in range(8):
                k.mm3(pf, uhi[:, kc, :], ulo[:, kc, :],
                      AHV[:, kc, :], ALV[:, kc, :],
                      start=(kc == 0), stop=False)
            for kc in range(16):
                tv = k.mm3(pf, chi[:, kc, :], clo[:, kc, :],
                           GUH[:, kc, :], GUL[:, kc, :],
                           start=False, stop=(kc == 15),
                           inc=k.sTN if kc == 15 else None)
            ws.done(slu, tv)
            ws.done(slc, tv)
            last_final_tv[0] = tv
            k.wait("vector", k.sTN, k.sTN.n)
            fv = k.vec(lambda e, pp=pf, oc=oc: e.tensor_scalar(
                bcw[:, 0:512], pp, p["ubcb_cols"][:, oc:oc + 1], None, OP.add))
            fin_evac[bank] = fv
            k.wait("tensor", k.sVE, fv)
            srcs = [bcw[:, j * 128:(j + 1) * 128] for j in range(4)]
            tcs = [FFV[:, j, oc * 128:(oc + 1) * 128] for j in range(4)]
            dst = ffn_tm.rearrange("pp (tc c) -> pp tc c", tc=4)[
                :, :, oc * 128:(oc + 1) * 128]
            tv2, ev2 = tgroup32(srcs, dst)
            fin_evac[bank] = max(fin_evac[bank], ev2)
        for tc in range(4):
            k.wait("vector", k.sVE, k.sVE.n)
            zv = k.vec(lambda e, tc=tc: e.tensor_tensor(
                z2b[:, :], FFV[:, tc, :], UV[:, tc, :], OP.add))
            k.vec(lambda e: e.reduce_sum(cc[:, 0:1], z2b[:, :], axis=AX.X))
            k.wait("scalar", k.sVE, zv + 1)
            sq = k.sca(lambda e, tc=tc: e.activation(
                UV[:, tc, 0:D], z2b[:, :], AF.Square, accum_out=cc[:, 1:2]))
            k.vec(lambda e: e.tensor_scalar(
                cc[:, 2:3], cc[:, 0:1], 1.0 / D, None, OP.mult))
            k.wait("vector", k.sSC, sq)
            k.vec(lambda e: e.tensor_scalar(
                cc[:, 3:4], cc[:, 1:2], 1.0 / D, None, OP.mult))
            k.vec(lambda e: e.tensor_tensor(
                cc[:, 4:5], cc[:, 2:3], cc[:, 2:3], OP.mult))
            k.vec(lambda e: e.scalar_tensor_tensor(
                cc[:, 5:6], cc[:, 4:5], -1.0, cc[:, 3:4], OP.mult, OP.add))
            vv = k.vec(lambda e: e.tensor_scalar(
                cc[:, 5:6], cc[:, 5:6], EPS, None, OP.add))
            k.wait("scalar", k.sVE, vv)
            sq2 = k.sca(lambda e: e.activation(cc[:, 6:7], cc[:, 5:6], AF.Sqrt))
            k.wait("vector", k.sSC, sq2)
            k.vec(lambda e: e.reciprocal(cc[:, 7:8], cc[:, 6:7]))
            uv = k.vec(lambda e, tc=tc: e.tensor_scalar(
                UV[:, tc, 0:D], z2b[:, :], cc[:, 2:3], cc[:, 7:8],
                OP.subtract, OP.mult))
            k.wait("gpsimd", k.sVE, uv)
            t0 = t0s + tc * 128
            out_store[(ts, tc)] = k.store(g["out"][t0:t0 + 128, :],
                                          UV[:, tc, 0:D], stOut)
    st.close()


def make_consts():
    import ml_dtypes
    c = {
        "c_identbf": np.eye(128, dtype=ml_dtypes.bfloat16),
        "c_ident32": np.eye(128, dtype=np.float32),
        "c_rank8": np.broadcast_to(
            np.arange(8, 0, -1, dtype=np.float32)[None, :], (128, 8)).copy(),
    }
    lam64 = np.float64(COST_LAMBDA)
    cost64 = np.array([2048.0 * h for h in HIDS], dtype=np.float64)
    v64 = -(lam64 * cost64)
    hi = v64.astype(np.float32)
    lo = (v64 - hi.astype(np.float64)).astype(np.float32)
    c["c_ones_bf"] = np.ones((1, 128), dtype=ml_dtypes.bfloat16)
    sel = np.zeros((9, 8 * 128), dtype=ml_dtypes.bfloat16)
    for e in range(8):
        sel[e, e * 128:(e + 1) * 128] = 1
    c["c_sel"] = sel
    c["c_ones512"] = np.ones((1, 512), dtype=np.float32)
    c["c_neglc_hi"] = hi.reshape(1, 8)
    c["c_neglc_lo"] = lo.reshape(1, 8)
    return c


def make_in_maps(inputs):
    x_full = np.ascontiguousarray(np.asarray(inputs["L0_attn_hidden_states"],
                                             dtype=np.float32))

    def row(v):
        return np.ascontiguousarray(np.asarray(v, dtype=np.float32)).reshape(1, -1)

    def mat(v):
        return np.ascontiguousarray(np.asarray(v, dtype=np.float32))

    base = {
        "w_qkv": mat(inputs["L0_attn_w_qkv"]), "b_qkv": row(inputs["L0_attn_b_qkv"]),
        "w_out": mat(inputs["L0_attn_w_out"]), "b_out": row(inputs["L0_attn_b_out"]),
        "down_w": mat(inputs["L0_ffn_down_w"]), "down_b": row(inputs["L0_ffn_down_b"]),
        "up_w": mat(inputs["L0_ffn_up_w"]), "up_b": row(inputs["L0_ffn_up_b"]),
        "router_w": mat(inputs["L0_ffn_router_w"]),
        "router_b": row(inputs["L0_ffn_router_b"]),
        "shared_w1": mat(inputs["L0_ffn_shared_w1"]),
        "shared_b1": row(inputs["L0_ffn_shared_b1"]),
        "shared_w2": mat(inputs["L0_ffn_shared_w2"]),
        "shared_b2": row(inputs["L0_ffn_shared_b2"]),
        "core_w": mat(inputs["L0_ffn_core_w"]), "core_b": row(inputs["L0_ffn_core_b"]),
    }
    for i in range(8):
        base[f"e{i}_w1"] = mat(inputs[f"L0_ffn_e{i}_w1"])
        base[f"e{i}_b1"] = row(inputs[f"L0_ffn_e{i}_b1"])
        base[f"e{i}_w2"] = mat(inputs[f"L0_ffn_e{i}_w2"])
        base[f"e{i}_b2"] = row(inputs[f"L0_ffn_e{i}_b2"])
    base.update(make_consts())
    return [dict(base, x=np.ascontiguousarray(x_full[c])) for c in range(NCORE)]


def kernel(**inputs):
    in_maps = make_in_maps(inputs)
    nc = build_program()
    res = run_bass_kernel_spmd(nc, in_maps, core_ids=list(range(NCORE)))
    out = np.stack([np.asarray(r["out"]) for r in res.results], axis=0)
    return out.astype(np.float32)
